# version 18 (final)
# baseline (speedup 1.0000x reference)
import numpy as np
import ml_dtypes
import concourse.bass as bass
import concourse.mybir as mybir
from concourse.bass_utils import run_bass_kernel_spmd

F32 = mybir.dt.float32
BF16 = mybir.dt.bfloat16
I8 = mybir.dt.int8
AF = mybir.ActivationFunctionType
ALU = mybir.AluOpType
AX = mybir.AxisListType

P = 128
B = 2
S = 8192
T = 2048
H = 2048
NH = 16
QL = 1536
KVL = 512
DN = 128
DR = 64
DV = 128
DQ = DN + DR
INTER = 8192
SCALE = float(1.0 / np.sqrt(DQ))
RMS_EPS = 1e-6
LN_EPS = 1e-5
B_OFF = 50.0
N_CORES = 8

BUILD_PHASES = 99
E_STOP = 3
SCRATCH_KIND = "Internal"
TAPS = ()


class Ctr:
    def __init__(self, nc, name):
        self.h = nc.alloc_semaphore(name)
        self.v = 0

    def inc(self, inst, k=1):
        inst.then_inc(self.h, k)
        self.v += k
        return inst


def _w(eng, sem, val=None):
    eng.wait_ge(sem.h, sem.v if val is None else val)


class GDma:

    def __init__(self, nc, name, nslots=2):
        self.s = [nc.alloc_semaphore(f"{name}_{j}") for j in range(nslots)]
        self.n = 0
        self.nslots = nslots

    def dma(self, eng, dst, src):
        j = self.n % self.nslots
        prev = 16 * (self.n // self.nslots)
        if prev > 0:
            eng.wait_ge(self.s[j], prev)
        inst = eng.dma_start(dst, src).then_inc(self.s[j], 16)
        self.n += 1
        return inst

    def wait(self, eng, k=None):
        k = self.n if k is None else k
        if k <= 0:
            return
        for j in range(self.nslots):
            cnt = (k - j + self.nslots - 1) // self.nslots
            if cnt > 0:
                eng.wait_ge(self.s[j], 16 * cnt)


def build_program():
    nc = bass.Bass("TRN2", target_bir_lowering=False, debug=False)
    t = nc.tensor
    sc = nc.scalar
    v = nc.vector
    g = nc.gpsimd
    s = nc.sync

    d_xf = nc.dram_tensor("x_full", [S, H], F32, kind="ExternalInput").ap()
    d_xo = nc.dram_tensor("x_own", [T, H], F32, kind="ExternalInput").ap()
    d_wqa = nc.dram_tensor("wqa", [16, P, QL], F32, kind="ExternalInput").ap()
    d_wqbn = nc.dram_tensor("wqbn", [NH, P, 12, DN], F32, kind="ExternalInput").ap()
    d_wqbp = nc.dram_tensor("wqbp", [P, 12, NH * DR], F32, kind="ExternalInput").ap()
    d_wkva = nc.dram_tensor("wkva", [16, P, 576], F32, kind="ExternalInput").ap()
    d_wkvb = nc.dram_tensor("wkvb", [NH, P, 4, 256], F32, kind="ExternalInput").ap()
    d_wot = nc.dram_tensor("wot", [16, P, H], F32, kind="ExternalInput").ap()
    d_qnw = nc.dram_tensor("qnw", [1, QL], F32, kind="ExternalInput").ap()
    d_kvnw = nc.dram_tensor("kvnw", [1, KVL], F32, kind="ExternalInput").ap()
    d_wg = nc.dram_tensor("wg", [64, P, 16, P], I8, kind="ExternalInput").ap()
    d_wu = nc.dram_tensor("wu", [64, P, 16, P], I8, kind="ExternalInput").ap()
    d_wd = nc.dram_tensor("wd", [64, P, H], I8, kind="ExternalInput").ap()
    d_sg = nc.dram_tensor("sg", [P, 64], F32, kind="ExternalInput").ap()
    d_su = nc.dram_tensor("su", [P, 64], F32, kind="ExternalInput").ap()
    d_sd = nc.dram_tensor("sd", [1, H], F32, kind="ExternalInput").ap()
    d_cosk = nc.dram_tensor("cosk", [P, S // P, 32], BF16, kind="ExternalInput").ap()
    d_sink = nc.dram_tensor("sink", [P, S // P, 32], BF16, kind="ExternalInput").ap()
    d_cosq = nc.dram_tensor("cosq", [P, T // P, NH, 32], F32,
                            kind="ExternalInput").ap()
    d_sinq = nc.dram_tensor("sinq", [P, T // P, NH, 32], F32,
                            kind="ExternalInput").ap()
    d_id = nc.dram_tensor("ident", [P, P], F32, kind="ExternalInput").ap()

    d_qfn = nc.dram_tensor("qfn_sp", [NH, P, T], BF16, kind=SCRATCH_KIND).ap()
    d_qfp = nc.dram_tensor("qfp_sp", [T // P, DR, NH, P], BF16,
                           kind=SCRATCH_KIND).ap()
    d_aosp = nc.dram_tensor("attn_sp", [NH, P, T], BF16, kind=SCRATCH_KIND).ap()
    d_kvc = nc.dram_tensor("kvc_sp", [P, 4, S], BF16, kind=SCRATCH_KIND).ap()
    d_kpe = nc.dram_tensor("kpe_sp", [DR, S], BF16, kind=SCRATCH_KIND).ap()
    d_h1 = nc.dram_tensor("h1_sp", [T // P, P, H], F32, kind=SCRATCH_KIND).ap()
    d_h1t = nc.dram_tensor("h1t_sp", [T // P, P, H], BF16, kind=SCRATCH_KIND).ap()

    d_tap = {}
    if "kvc" in TAPS:
        d_tap["kvc"] = nc.dram_tensor("tap_kvc", [P, 4, S], BF16,
                                      kind="ExternalOutput").ap()
    if "kpe" in TAPS:
        d_tap["kpe"] = nc.dram_tensor("tap_kpe", [DR, S], BF16,
                                      kind="ExternalOutput").ap()
    if "qlora" in TAPS:
        d_tap["qlora"] = nc.dram_tensor("tap_qlora", [P, 12, T], BF16,
                                        kind="ExternalOutput").ap()

    d_out = nc.dram_tensor("out", [T, H], F32, kind="ExternalOutput").ap()

    NKS = S // P
    NQT = T // P

    ctx_stack = []

    def alloc(*args):
        cm = nc.sbuf_tensor(*args)
        hnd = cm.__enter__()
        ctx_stack.append(cm)
        return hnd

    def alloc_ps(*args):
        cm = nc.psum_tensor(*args)
        hnd = cm.__enter__()
        ctx_stack.append(cm)
        return hnd

    def free_to(n):
        while len(ctx_stack) > n:
            ctx_stack.pop().__exit__(None, None, None)

    tap_sem = Ctr(nc, "tap")

    ident = alloc("ident_bf", [P, P], BF16)
    ones_bf = alloc("ones_bf", [P, 1], BF16)
    ones_f = alloc("ones_f", [P, 1], F32)
    onesr_f = alloc("onesr_f", [1, P], F32)
    sd_bc = alloc("sd_bc", [P, H], F32)
    sg_sb = alloc("sg_sb", [P, 64], F32)
    su_sb = alloc("su_sb", [P, 64], F32)
    eps_r = alloc("eps_r", [P, 1], F32)
    eps_l = alloc("eps_l", [P, 1], F32)

    ld = Ctr(nc, "ld")
    gld = Ctr(nc, "gld")
    cst = Ctr(nc, "cst")

    g.dma_start(ident[:], d_id).then_inc(gld.h, 16)
    gld.v += 16
    with (nc.sbuf_tensor("c_sd", [1, H], F32) as c_sd,
          nc.psum_tensor("c_ps", [P, 512], F32) as c_ps):
        for dst, src in ((c_sd[:], d_sd),
                         (sg_sb[:], d_sg), (su_sb[:], d_su)):
            s.dma_start(dst, src).then_inc(ld.h, 16)
            ld.v += 16
        v.memset(ones_bf[:], 1.0)
        v.memset(ones_f[:], 1.0)
        v.memset(eps_r[:], RMS_EPS)
        v.memset(eps_l[:], LN_EPS)
        cst.inc(v.memset(onesr_f[:], 1.0))
        _w(t, ld)
        _w(t, gld)
        _w(t, cst)
        for dst, src, n in ((sd_bc, c_sd, H),):
            for j in range(0, n, 512):
                wdt = min(512, n - j)
                _w(t, cst)
                cst.inc(t.matmul(c_ps[:, 0:wdt], onesr_f[:], src[0:1, j:j + wdt],
                                 start=True, stop=True))
                _w(v, cst)
                cst.inc(v.tensor_copy(dst[:, j:j + wdt], c_ps[:, 0:wdt]))
        nc.all_engine_barrier()
    consts_cst = cst.v

    base = len(ctx_stack)

    if BUILD_PHASES >= 2:
        _snap = nc.snapshot_sems()
        q_lora = alloc("q_lora", [P, 12, T], BF16)
        qnw_bc = alloc("qnw_bc", [P, QL], F32)
        c_qnw = alloc("b_qnw", [1, QL], F32)
        wqa = alloc("wqa_sb", [P, 16, QL], BF16)
        xw = [alloc(f"b_xw{i}", [P, H], BF16) for i in range(3)]
        xT = [alloc(f"b_xT{i}", [P, 16, P], BF16) for i in range(3)]
        qltmp = [alloc(f"qltmp{i}", [P, QL], F32) for i in range(2)]
        qltok = [alloc(f"qltok{i}", [P, QL], BF16) for i in range(2)]
        stat = alloc("b_stat", [P, 8], F32)
        sqj = alloc("b_sqj", [P, 512], BF16)

        ql_ps = [alloc_ps(f"ql_ps{i}", [P, 512], F32) for i in range(2)]
        trx_ps = [alloc_ps(f"trx_ps{i}", [P, 2048], BF16) for i in range(2)]
        trb_ps = alloc_ps("trb_ps", [P, QL], BF16)

        xl = GDma(nc, "b_xl", 3)
        wl = Ctr(nc, "b_wl")
        xtr = Ctr(nc, "b_xtr")
        tev = Ctr(nc, "b_tev")
        qlm = Ctr(nc, "b_qlm")
        qcp = Ctr(nc, "b_qcp")
        rsd = Ctr(nc, "b_rsd")
        vap = Ctr(nc, "b_vap")
        qtr = Ctr(nc, "b_qtr")
        qev = Ctr(nc, "b_qev")
        sch = Ctr(nc, "b_sch")
        vch = Ctr(nc, "b_vch")

        cs2 = Ctr(nc, "b_cs2")
        wlg = Ctr(nc, "b_wlg")
        vap_at = {}

        def emit_qltr(i):
            _w(t, vap, vap_at[i])
            if i >= 1:
                _w(t, qev, i)
            for j in range(12):
                mm = t.transpose(trb_ps[:, j * P:(j + 1) * P],
                                 qltok[i % 2][:, j * P:(j + 1) * P], ident[:])
            qtr.inc(mm)
            _w(v, qtr)
            qev.inc(v.tensor_copy(
                q_lora[:, :, i * P:(i + 1) * P],
                trb_ps.ap().rearrange("p (j f) -> p j f", j=12)))
        g.dma_start(wqa[:], d_wqa.rearrange("a p b -> p a b")).then_inc(wlg.h, 16)
        wlg.v += 16
        s.dma_start(c_qnw[:], d_qnw).then_inc(wl.h, 16)
        wl.v += 16
        _w(t, wl, wl.v)
        _w(t, wlg, wlg.v)
        for seg in range(3):
            _w(t, cs2)
            cs2.inc(t.matmul(ql_ps[0][:], onesr_f[:],
                             c_qnw[0:1, seg * 512:(seg + 1) * 512],
                             start=True, stop=True))
            _w(v, cs2)
            cs2.inc(v.tensor_copy(qnw_bc[:, seg * 512:(seg + 1) * 512],
                                  ql_ps[0][:]))

        def xpath(i):
            bx = i % 3
            if i >= 3:
                _w(g, xtr, i - 2)
            xl.dma(g, xw[bx][:], d_xo[i * P:(i + 1) * P, :])
            xl.wait(t, i + 1)
            if i >= 2:
                _w(t, tev, i - 1)
            if i == 0:
                _w(t, ld)
                _w(t, cst, consts_cst)
            for dc in range(16):
                mm = t.transpose(trx_ps[i % 2][:, dc * P:(dc + 1) * P],
                                 xw[bx][:, dc * P:(dc + 1) * P], ident[:])
            xtr.inc(mm)
            _w(v, xtr, i + 1)
            if i >= 3:
                _w(v, qlm, 3 * (i - 2))
            tev.inc(v.tensor_copy(
                xT[bx][:],
                trx_ps[i % 2].ap().rearrange("p (dc f) -> p dc f", dc=16)))

        xpath(0)
        xpath(1)
        for i in range(NQT):
            bx = i % 3
            if i + 2 < NQT:
                xpath(i + 2)
            _w(t, tev, i + 1)
            if i == 0:
                _w(t, cs2, 6)
            for seg in range(3):
                sg_i = i * 3 + seg
                if sg_i >= 2:
                    _w(t, qcp, sg_i - 1)
                for kc in range(16):
                    mm = t.matmul(ql_ps[sg_i % 2][:], xT[bx][:, kc, :],
                                  wqa[:, kc, seg * 512:(seg + 1) * 512],
                                  start=(kc == 0), stop=(kc == 15))
                qlm.inc(mm)
                _w(sc, qlm)
                if i >= 2 and seg == 0:
                    _w(sc, vap, i - 1)
                sc.copy(qltmp[i % 2][:, seg * 512:(seg + 1) * 512],
                        ql_ps[sg_i % 2][:])
                _w(sc, qcp)
                qcp.inc(sc.activation(sqj[:], ql_ps[sg_i % 2][:], AF.Square,
                                      accum_out=stat[:, seg:seg + 1]))
            _w(sc, qcp)
            if i >= 1:
                _w(sc, vch, i)
            sch.inc(sc.activation(stat[:, 3:4], stat[:, 0:1], AF.Identity,
                                  bias=stat[:, 1:2]))
            _w(sc, sch)
            sch.inc(sc.activation(stat[:, 4:5], stat[:, 3:4], AF.Identity,
                                  bias=stat[:, 2:3]))
            _w(sc, sch)
            sch.inc(sc.activation(stat[:, 5:6], stat[:, 4:5], AF.Ln,
                                  bias=eps_r[:, 0:1], scale=1.0 / QL))
            _w(sc, sch)
            rsd.inc(sc.activation(stat[:, 6:7], stat[:, 5:6], AF.Exp,
                                  scale=-0.5))
            _w(v, rsd)
            if i == 0:
                _w(v, cs2, 6)
            if i >= 2:
                _w(v, qtr, i - 1)
            vch.inc(v.tensor_scalar(qltmp[i % 2][:], qltmp[i % 2][:],
                                    stat[:, 6:7], None, ALU.mult))
            _w(v, vch)
            vap.inc(v.tensor_tensor(qltok[i % 2][:], qltmp[i % 2][:], qnw_bc[:],
                                    ALU.mult))
            vap_at[i] = vap.v
            if i >= 1:
                emit_qltr(i - 1)

        emit_qltr(NQT - 1)

        if "qlora" in d_tap:
            _w(s, qev)
            s.dma_start(d_tap["qlora"], q_lora[:]).then_inc(tap_sem.h, 16)
            tap_sem.v += 16

        xl.wait(s)
        s.wait_ge(wl.h, wl.v)
        s.wait_ge(wlg.h, wlg.v)
        s.wait_ge(tap_sem.h, tap_sem.v)
        nc.all_engine_barrier()
        free_to(base + 1)

        wqbn = [alloc(f"wqbn{i}", [P, 12, DN], BF16) for i in range(2)]
        qfnb = [alloc(f"qfnb{i}", [P, T], BF16) for i in range(2)]
        wqbp = alloc("wqbp_sb", [P, 12, NH * DR], BF16)
        qptok = [alloc(f"qptok{i}", [P, NH * DR], BF16) for i in range(2)]
        qfpst = [alloc(f"qfpst{i}", [DR, NH * P], BF16) for i in range(2)]
        cosq = [alloc(f"cosq{i}", [P, NH, 32], F32) for i in range(2)]
        sinq = [alloc(f"sinq{i}", [P, NH, 32], F32) for i in range(2)]
        rope1 = alloc("p_rope1", [P, NH, 32], F32)
        rope2 = alloc("p_rope2", [P, NH, 32], F32)
        rope3 = alloc("p_rope3", [P, NH, 32], F32)
        rope4 = alloc("p_rope4", [P, NH, 32], F32)

        qn_ps = [alloc_ps(f"qn_ps{i}", [P, 512], F32) for i in range(2)]
        qp_ps = [alloc_ps(f"qp_ps{i}", [P, NH * DR], F32) for i in range(2)]
        trp_ps = alloc_ps("trp_ps", [DR, NH * P], BF16)

        wl = Ctr(nc, "p_wl")
        wbn = GDma(nc, "p_wbn", 2)
        tbl = GDma(nc, "p_tbl", 4)
        spn = GDma(nc, "p_spn", 2)
        spp = GDma(nc, "p_spp", 2)
        qnm = Ctr(nc, "p_qnm")
        qne = Ctr(nc, "p_qne")
        spl = Ctr(nc, "p_spl")
        qpm = Ctr(nc, "p_qpm")
        rpd = Ctr(nc, "p_rpd")
        ptr = Ctr(nc, "p_ptr")
        pev = Ctr(nc, "p_pev")
        pch = Ctr(nc, "p_pch")

        g.dma_start(wqbp[:], d_wqbp).then_inc(wl.h, 16)
        wl.v += 16
        wbn.dma(g, wqbn[0][:], d_wqbn[0])

        for h in range(NH):
            if h + 1 < NH:
                if h >= 1:
                    _w(g, qnm, h * (T // 512))
                wbn.dma(g, wqbn[(h + 1) % 2][:], d_wqbn[h + 1])
            if h == 0:
                _w(t, wl, 16)
            wbn.wait(t, h + 1)
            if h >= 2:
                spn.wait(t, h - 1)
            for win in range(T // 512):
                iw = h * (T // 512) + win
                if iw >= 2:
                    _w(t, qne, iw - 1)
                for kc in range(12):
                    mm = t.matmul(qn_ps[iw % 2][:],
                                  wqbn[h % 2][:, kc, :],
                                  q_lora[:, kc, win * 512:(win + 1) * 512],
                                  start=(kc == 0), stop=(kc == 11))
                qnm.inc(mm)
                _w(sc, qnm)
                qne.inc(sc.copy(qfnb[h % 2][:, win * 512:(win + 1) * 512],
                                qn_ps[iw % 2][:]))
            _w(s, qne)
            spn.dma(s, d_qfn[h], qfnb[h % 2][:])

        tbl.dma(s, cosq[0][:], d_cosq[:, 0, :, :])
        tbl.dma(s, sinq[0][:], d_sinq[:, 0, :, :])

        for ts in range(NQT):
            if ts + 1 < NQT:
                if ts >= 1:
                    _w(s, rpd, ts)
                tbl.dma(s, cosq[(ts + 1) % 2][:], d_cosq[:, ts + 1, :, :])
                tbl.dma(s, sinq[(ts + 1) % 2][:], d_sinq[:, ts + 1, :, :])
            _w(t, wl, 16)
            if ts >= 2:
                _w(t, rpd, ts - 1)
            for kc in range(12):
                t.matmul(qp_ps[ts % 2][:, 0:512],
                         q_lora[:, kc, ts * P:(ts + 1) * P],
                         wqbp[:, kc, 0:512], start=(kc == 0), stop=(kc == 11))
            for kc in range(12):
                mm = t.matmul(qp_ps[ts % 2][:, 512:1024],
                              q_lora[:, kc, ts * P:(ts + 1) * P],
                              wqbp[:, kc, 512:1024], start=(kc == 0),
                              stop=(kc == 11))
            qpm.inc(mm)
            _w(v, qpm)
            tbl.wait(v, 2 * (ts + 1))
            pe = qp_ps[ts % 2].ap().rearrange("p (nh d two) -> p nh two d",
                                              two=2, nh=NH)
            qp = qptok[ts % 2].ap().rearrange("p (nh d two) -> p nh two d",
                                              two=2, nh=NH)
            ce = cosq[ts % 2][:]
            se = sinq[ts % 2][:]
            v.tensor_tensor(rope1[:], pe[:, :, 0, :], ce, ALU.mult)
            pch.inc(v.tensor_tensor(rope2[:], pe[:, :, 1, :], se, ALU.mult))
            _w(v, pch)
            v.tensor_sub(qp[:, :, 0, :], rope1[:], rope2[:])
            v.tensor_tensor(rope3[:], pe[:, :, 0, :], se, ALU.mult)
            rpd.inc(v.tensor_tensor(rope4[:], pe[:, :, 1, :], ce, ALU.mult))
            _w(v, rpd)
            qpm.inc(v.tensor_add(qp[:, :, 1, :], rope3[:], rope4[:]))
            _w(t, qpm)
            if ts >= 2:
                _w(t, pev, ts - 1)
            for h in range(NH):
                mm = t.transpose(trp_ps[:, h * P:(h + 1) * P],
                                 qptok[ts % 2][:, h * DR:(h + 1) * DR], ident[:])
            ptr.inc(mm)
            _w(v, ptr)
            if ts >= 2:
                spp.wait(v, ts - 1)
            pev.inc(v.tensor_copy(qfpst[ts % 2][:], trp_ps[:]))
            _w(s, pev)
            spp.dma(s, d_qfp[ts],
                    qfpst[ts % 2].ap().rearrange("p (nh f) -> p nh f", nh=NH))

        spn.wait(s)
        spp.wait(s)
        s.wait_ge(wl.h, wl.v)
        tbl.wait(s)
        nc.all_engine_barrier()
        free_to(base)
        nc.clear_and_free_semaphores(nc.allocated_since(_snap))
        nc.all_engine_barrier()

    if BUILD_PHASES >= 1:
        _snap = nc.snapshot_sems()
        kv_c = alloc("kv_c", [P, 4, S], BF16)
        k_pe = alloc("k_pe", [P, S], BF16)
        kvw_bc = alloc("kvw_bc", [P, KVL], F32)
        c_kvnw = alloc("a_kvnw", [1, KVL], F32)
        wkva = alloc("wkva_sb", [P, 16, 576], BF16)
        cosk = alloc("cosk_sb", [P, S // P, 32], BF16)
        sink = alloc("sink_sb", [P, S // P, 32], BF16)
        xw = [alloc(f"xw{i}", [P, H], BF16) for i in range(3)]
        xT = [alloc(f"xT{i}", [P, 16, P], BF16) for i in range(3)]
        kvtok = [alloc(f"kvtok{i}", [P, KVL], BF16) for i in range(2)]
        kpet = [alloc(f"kpet{i}", [P, DR], BF16) for i in range(2)]
        tmpf = alloc("tmpf", [P, KVL], F32)
        rope1 = alloc("rope1", [P, 32], F32)
        rope2 = alloc("rope2", [P, 32], F32)
        rope3 = alloc("rope3", [P, 32], F32)
        rope4 = alloc("rope4", [P, 32], F32)
        stat = alloc("stat", [P, 4], F32)
        sqj = alloc("sqj", [P, 512], BF16)

        kv_ps = [alloc_ps(f"kv_ps{i}", [P, 576], F32) for i in range(2)]
        tr_ps = [alloc_ps(f"tr_ps{i}", [P, 2048], BF16) for i in range(2)]

        xl = GDma(nc, "a_xl", 3)
        wl = Ctr(nc, "a_wl")
        xtr = Ctr(nc, "a_xtr")
        tev = Ctr(nc, "a_tev")
        kvm = Ctr(nc, "a_kvm")
        tevx = Ctr(nc, "a_tevx")
        rsd = Ctr(nc, "a_rsd")
        sqd = Ctr(nc, "a_sqd")
        vps = Ctr(nc, "a_vps")
        vrd = Ctr(nc, "a_vrd")
        vch = Ctr(nc, "a_vch")

        cs2 = Ctr(nc, "a_cs2")
        wlg = Ctr(nc, "a_wlg")
        g.dma_start(wkva[:], d_wkva.rearrange("a p b -> p a b")).then_inc(wlg.h, 16)
        wlg.v += 16
        s.dma_start(cosk[:], d_cosk).then_inc(wl.h, 16)
        s.dma_start(sink[:], d_sink).then_inc(wl.h, 16)
        s.dma_start(c_kvnw[:], d_kvnw).then_inc(wl.h, 16)
        wl.v += 48
        _w(t, wl, wl.v)
        _w(t, wlg, wlg.v)
        _w(t, cst, consts_cst)
        cs2.inc(t.matmul(kv_ps[0][:, 0:512], onesr_f[:], c_kvnw[0:1, :],
                         start=True, stop=True))
        _w(v, cs2)
        cs2.inc(v.tensor_copy(kvw_bc[:], kv_ps[0][:, 0:512]))

        gctr = [0]
        gx = {}
        gt = {}
        tr_groups = []
        evx_at = {}

        def emit_trc(i):
            bx2 = i % 2
            k = gctr[0]
            gctr[0] += 1
            gt[i] = k
            _w(t, vrd, i + 1)
            if k >= 2:
                c2, v2 = tr_groups[k - 2]
                t.wait_ge(c2.h, v2)
            for j in range(4):
                t.transpose(tr_ps[k % 2][:, j * P:(j + 1) * P],
                            kvtok[bx2][:, j * P:(j + 1) * P], ident[:])
            mm = t.transpose(tr_ps[k % 2][0:DR, 4 * P:5 * P],
                             kpet[bx2][:], ident[:])
            xtr.inc(mm)
            _w(v, xtr)
            pos = i * P
            v.tensor_copy(kv_c[:, :, pos:pos + P],
                          tr_ps[k % 2].ap()[:, 0:512].rearrange(
                              "p (j f) -> p j f", j=4))
            v.tensor_copy(k_pe[0:DR, pos:pos + P],
                          tr_ps[k % 2].ap()[0:DR, 4 * P:5 * P])
            tev.inc(v.tensor_copy(k_pe[DR:P, pos:pos + P],
                                  tr_ps[k % 2].ap()[0:DR, 4 * P:5 * P]))
            tr_groups.append((tev, tev.v))

        def xpath(i):
            bx = i % 3
            if i >= 3:
                _w(g, xtr, gx[i - 3] + 1)
            xl.dma(g, xw[bx][:], d_xf[i * P:(i + 1) * P, :])
            k = gctr[0]
            gctr[0] += 1
            gx[i] = k
            xl.wait(t, i + 1)
            if k >= 2:
                c2, v2 = tr_groups[k - 2]
                t.wait_ge(c2.h, v2)
            if k == 0:
                _w(t, gld)
                _w(t, cst, consts_cst)
            for dc in range(16):
                mm = t.transpose(tr_ps[k % 2][:, dc * P:(dc + 1) * P],
                                 xw[bx][:, dc * P:(dc + 1) * P], ident[:])
            xtr.inc(mm)
            _w(sc, xtr, xtr.v)
            if i >= 3:
                _w(sc, kvm, i - 2)
            tevx.inc(sc.copy(
                xT[bx][:],
                tr_ps[k % 2].ap().rearrange("p (dc f) -> p dc f", dc=16)))
            tr_groups.append((tevx, tevx.v))
            evx_at[i] = tevx.v

        xpath(0)
        xpath(1)
        for i in range(NKS):
            bx3 = i % 3
            bx = i % 2
            if i + 2 < NKS:
                xpath(i + 2)
            _w(t, tevx, evx_at[i])
            if i == 0:
                _w(t, cs2, 2)
            if i >= 2:
                _w(t, sqd, 2 * i - 3)
                _w(t, vps, i - 1)
            for kc in range(16):
                t.matmul(kv_ps[bx][:, 0:512], xT[bx3][:, kc, :],
                         wkva[:, kc, 0:512], start=(kc == 0), stop=(kc == 15))
            for kc in range(16):
                mm = t.matmul(kv_ps[bx][:, 512:576], xT[bx3][:, kc, :],
                              wkva[:, kc, 512:576],
                              start=(kc == 0), stop=(kc == 15))
            kvm.inc(mm)
            if i >= 1:
                emit_trc(i - 1)
            _w(sc, kvm, i + 1)
            if i >= 1:
                _w(sc, vch, 2 * i - 1)
            sqd.inc(sc.activation(sqj[:], kv_ps[bx][:, 0:512], AF.Square,
                                  accum_out=stat[:, 0:1]))
            _w(sc, sqd)
            sqd.inc(sc.activation(stat[:, 1:2], stat[:, 0:1], AF.Ln,
                                  bias=eps_r[:, 0:1], scale=1.0 / KVL))
            _w(sc, sqd)
            rsd.inc(sc.activation(stat[:, 2:3], stat[:, 1:2], AF.Exp,
                                  scale=-0.5))
            _w(v, rsd)
            if i == 0:
                _w(v, wl, 48)
                _w(v, cs2, 2)
            if i >= 2:
                _w(v, xtr, gt[i - 2] + 1)
            vch.inc(v.tensor_scalar(tmpf[:], kv_ps[bx][:, 0:512],
                                    stat[:, 2:3], None, ALU.mult))
            _w(v, vch)
            v.tensor_tensor(kvtok[bx][:], tmpf[:], kvw_bc[:], ALU.mult)
            pe = kv_ps[bx].ap()[:, 512:576].rearrange("p (d two) -> p two d",
                                                      two=2)
            kp = kpet[bx].ap().rearrange("p (d two) -> p two d", two=2)
            ce = cosk[:, i, :]
            se = sink[:, i, :]
            v.tensor_tensor(rope1[:], pe[:, 0, :], ce, ALU.mult)
            vch.inc(v.tensor_tensor(rope2[:], pe[:, 1, :], se, ALU.mult))
            _w(v, vch)
            v.tensor_sub(kp[:, 0, :], rope1[:], rope2[:])
            v.tensor_tensor(rope3[:], pe[:, 0, :], se, ALU.mult)
            vps.inc(v.tensor_tensor(rope4[:], pe[:, 1, :], ce, ALU.mult))
            _w(v, vps)
            vrd.inc(v.tensor_add(kp[:, 1, :], rope3[:], rope4[:]))
        emit_trc(NKS - 1)

        if "kvc" in d_tap:
            _w(s, tev)
            _w(s, tevx)
            s.dma_start(d_tap["kvc"], kv_c[:]).then_inc(tap_sem.h, 16)
            tap_sem.v += 16
        if "kpe" in d_tap:
            _w(s, tev)
            s.dma_start(d_tap["kpe"], k_pe[:]).then_inc(tap_sem.h, 16)
            tap_sem.v += 16

        xl.wait(s)
        s.wait_ge(wl.h, wl.v)
        s.wait_ge(wlg.h, wlg.v)
        s.wait_ge(tap_sem.h, tap_sem.v)
        nc.all_engine_barrier()
        free_to(base + 2)
        nc.clear_and_free_semaphores(nc.allocated_since(_snap))
        nc.all_engine_barrier()

    if BUILD_PHASES >= 3:
        _snap = nc.snapshot_sems()
        kf = alloc("kf_sb", [P, S], BF16)
        Vt = alloc("V_sb", [P, NKS, DV], BF16)
        HF = T // 2
        NCK = (HF + 511) // 512
        qfn = [alloc(f"qfn{i}", [P, T], BF16) for i in range(2)]
        qfp = [alloc(f"qfp{i}", [P, T], BF16) for i in range(2)]
        atb = alloc("at_sb", [P, 4, HF], BF16)
        sqb = [alloc(f"sqb{i}", [P, 512], BF16) for i in range(2)]
        sqp = [alloc(f"sqp{i}", [DR, 512], BF16) for i in range(2)]
        rb = alloc("rb_sb", [P, HF], F32)
        lac = alloc("lac_sb", [P, 2, HF], F32)
        aob = [alloc(f"aob{i}", [P, T], BF16) for i in range(2)]
        rl = alloc("rl_sb", [1, HF], F32)
        bias = alloc("bias_sb", [P, 1], F32)
        mstat = alloc("mstat", [1, 8], F32)
        kpe2s = alloc("kpe2s", [DR, S], BF16)
        wkvb = [alloc(f"wkvb{i}", [P, 4, 256], BF16) for i in range(2)]

        scp = [alloc_ps(f"scp{i}", [P, max(HF, 512)], F32) for i in range(2)]
        avp = alloc_ps("avp", [P, max(HF, 512)], F32)
        lp = alloc_ps("lp", [P, max(HF, 512)], F32)

        wlc = GDma(nc, "c_wl", 2)
        qld = GDma(nc, "c_qld", 6)
        kfm = Ctr(nc, "c_kfm")
        kfe = Ctr(nc, "c_kfe")
        k2m = Ctr(nc, "c_k2m")
        k2d = Ctr(nc, "c_k2d")
        vvm = Ctr(nc, "c_vvm")
        vve = Ctr(nc, "c_vve")
        qsq = Ctr(nc, "c_qsq")
        bsd = Ctr(nc, "c_bsd")
        scm = Ctr(nc, "c_scm")
        exd = Ctr(nc, "c_exd")
        pat = Ctr(nc, "c_pat")
        ladc = Ctr(nc, "c_ladc")
        lrm = Ctr(nc, "c_lrm")
        rcd = Ctr(nc, "c_rcd")
        bcd = Ctr(nc, "c_bcd")
        rbd = Ctr(nc, "c_rbd")
        aod = Ctr(nc, "c_aod")
        avm = Ctr(nc, "c_avm")
        aom = Ctr(nc, "c_aom")
        cch = Ctr(nc, "c_cch")
        kq = Ctr(nc, "c_kq")
        vsq = Ctr(nc, "c_vsq")
        spl = GDma(nc, "c_spl", 2)
        carry_posts = []

        wlc.dma(g, wkvb[0][:], d_wkvb[0])
        qld.dma(s, qfn[0][:], d_qfn[0])
        qld.dma(s, qfp[0].ap()[0:DR].rearrange("p (n f) -> p n f", f=P),
                d_qfp[:, :, 0, :].rearrange("n p f -> p n f"))
        qld.dma(s, qfp[0].ap()[DR:P].rearrange("p (n f) -> p n f", f=P),
                d_qfp[:, :, 0, :].rearrange("n p f -> p n f"))

        for h in range(NH):
            hb = h % 2
            if h + 1 < NH:
                if h >= 1:
                    _w(g, vvm, h * (S // 512))
                wlc.dma(g, wkvb[(h + 1) % 2][:], d_wkvb[h + 1])
                if h >= 1:
                    _w(s, scm, h * 2 * NKS)
                    _w(s, vsq, vsq.v)
                qld.dma(s, qfn[(h + 1) % 2][:], d_qfn[h + 1])
                qld.dma(s,
                        qfp[(h + 1) % 2].ap()[0:DR].rearrange(
                            "p (n f) -> p n f", f=P),
                        d_qfp[:, :, h + 1, :].rearrange("n p f -> p n f"))
                qld.dma(s,
                        qfp[(h + 1) % 2].ap()[DR:P].rearrange(
                            "p (n f) -> p n f", f=P),
                        d_qfp[:, :, h + 1, :].rearrange("n p f -> p n f"))

            vsq_at = {}
            wlc.wait(t, h + 1)
            if h == 0:
                for tc in range(S // 512):
                    kq.inc(v.tensor_tensor(kpe2s[:, tc * 512:(tc + 1) * 512],
                                           k_pe[0:DR, tc * 512:(tc + 1) * 512],
                                           k_pe[0:DR, tc * 512:(tc + 1) * 512],
                                           ALU.mult))

            def emit_ones_k(tc):
                _w(t, vsq, vsq_at[tc])
                _w(t, k2d)
                if tc == 0 and h >= 1:
                    _w(t, rbd)
                t.matmul(lp[0:1, 0:512], ones_bf[:], sqb[tc % 2][:],
                         start=True, stop=False)
                k2m.inc(t.matmul(lp[0:1, 0:512], ones_bf[0:DR, :],
                                 kpe2s[:, tc * 512:(tc + 1) * 512],
                                 start=False, stop=True))
                _w(v, k2m)
                _w(v, k2d)
                if tc == 0:
                    k2d.inc(v.reduce_max(mstat[:, 0:1], lp[0:1, 0:512],
                                         axis=AX.X))
                else:
                    cch.inc(v.reduce_max(mstat[:, 1:2], lp[0:1, 0:512],
                                         axis=AX.X))
                    _w(v, cch)
                    k2d.inc(v.tensor_max(mstat[:, 0:1], mstat[:, 0:1],
                                         mstat[:, 1:2]))

            for tc in range(S // 512):
                it = h * (S // 512) + tc
                if tc == 1 and carry_posts:
                    carry_posts.pop(0)()
                if tc == max(2, min(4, S // 512 - 1)) and carry_posts:
                    carry_posts.pop(0)()
                if it >= 2:
                    _w(t, kfe, it - 1)
                if tc < 2 and h >= 1:
                    _w(t, exd)
                for kc in range(4):
                    mm = t.matmul(scp[tc % 2][:, 0:512], wkvb[hb][:, kc, 0:DN],
                                  kv_c[:, kc, tc * 512:(tc + 1) * 512],
                                  start=(kc == 0), stop=(kc == 3))
                kfm.inc(mm)
                _w(sc, kfm)
                if tc < 2 and h >= 1:
                    _w(sc, scm)
                kfe.inc(sc.copy(kf[:, tc * 512:(tc + 1) * 512],
                                scp[tc % 2][:, 0:512]))
                _w(v, kfe)
                if h == 0 and tc == 0:
                    _w(v, kq)
                if tc >= 2:
                    _w(v, k2m)
                elif h >= 1:
                    _w(v, k2m)
                vsq.inc(v.tensor_tensor(sqb[tc % 2][:],
                                        kf[:, tc * 512:(tc + 1) * 512],
                                        kf[:, tc * 512:(tc + 1) * 512],
                                        ALU.mult))
                vsq_at[tc] = vsq.v
                if tc >= 1:
                    emit_ones_k(tc - 1)
            emit_ones_k(S // 512 - 1)

            for vg in range(S // 512):
                it = h * (S // 512) + vg
                if it >= 2:
                    _w(t, vve, it - 1)
                if vg < 2:
                    _w(t, kfe)
                    if h >= 1:
                        _w(t, pat)
                for sl in range(4):
                    for kc in range(4):
                        mm = t.matmul(
                            scp[vg % 2][:, sl * P:(sl + 1) * P],
                            kv_c[:, kc, (vg * 4 + sl) * P:(vg * 4 + sl + 1) * P],
                            wkvb[hb][:, kc, DN:256],
                            start=(kc == 0), stop=(kc == 3))
                vvm.inc(mm)
                _w(v, vvm)
                vve.inc(v.tensor_copy(
                    Vt[:, vg * 4:(vg + 1) * 4, :],
                    scp[vg % 2].ap()[:, 0:512].rearrange("p (sl f) -> p sl f", sl=4)))

            qld.wait(v, 3 * (h + 1))
            for qc in range(T // 512):
                if qc < 2:
                    _w(v, k2m)
                else:
                    _w(v, k2m, k2m.v - 1)
                v.tensor_tensor(sqb[qc % 2][:],
                                qfn[hb][:, qc * 512:(qc + 1) * 512],
                                qfn[hb][:, qc * 512:(qc + 1) * 512], ALU.mult)
                vsq.inc(v.tensor_tensor(sqp[qc % 2][:],
                                        qfp[hb][0:DR, qc * 512:(qc + 1) * 512],
                                        qfp[hb][0:DR, qc * 512:(qc + 1) * 512],
                                        ALU.mult))
                _w(t, vsq)
                _w(t, k2d)
                t.matmul(lp[0:1, 0:512], ones_bf[:], sqb[qc % 2][:],
                         start=True, stop=False)
                k2m.inc(t.matmul(lp[0:1, 0:512], ones_bf[0:DR, :], sqp[qc % 2][:],
                                 start=False, stop=True))
                _w(v, k2m)
                _w(v, k2d)
                if qc == 0:
                    k2d.inc(v.reduce_max(mstat[:, 2:3], lp[0:1, 0:512],
                                         axis=AX.X))
                else:
                    cch.inc(v.reduce_max(mstat[:, 3:4], lp[0:1, 0:512],
                                         axis=AX.X))
                    _w(v, cch)
                    k2d.inc(v.tensor_max(mstat[:, 2:3], mstat[:, 2:3],
                                         mstat[:, 3:4]))

            _w(v, k2d)
            if h >= 1:
                _w(v, exd)
            k2d.inc(v.tensor_tensor(mstat[:, 4:5], mstat[:, 0:1], mstat[:, 2:3],
                                    ALU.mult))
            _w(sc, k2d)
            qsq.inc(sc.activation(mstat[:, 5:6], mstat[:, 4:5], AF.Sqrt,
                                  scale=1.0404))
            _w(t, qsq)
            _w(t, k2d)
            k2m.inc(t.matmul(lp[:, 0:1], onesr_f[:], mstat[0:1, 5:6],
                             start=True, stop=True))
            _w(v, k2m)
            bsd.inc(v.tensor_scalar(bias[:], lp[:, 0:1], -SCALE, B_OFF,
                                    ALU.mult, ALU.add))

            cks = [(o, min(512, HF - o)) for o in range(0, HF, 512)]
            items = [(qh, ks) for qh in range(2) for ks in range(NKS)]

            def emit_nope(j):
                qh, ks = items[j]
                q0 = qh * HF
                idx = h * 2 * NKS + j
                if idx >= 2:
                    _w(t, exd, idx - 1)
                if j < 2:
                    _w(t, vve)
                for o, wd in cks:
                    t.matmul(scp[idx % 2][:, o:o + wd],
                             kf[:, ks * P:(ks + 1) * P],
                             qfn[hb][:, q0 + o:q0 + o + wd],
                             start=True, stop=False)

            def emit_pe(j):
                qh, ks = items[j]
                q0 = qh * HF
                idx = h * 2 * NKS + j
                if HF == 1024:
                    t.matmul(scp[idx % 2][:, 0:512],
                             k_pe[0:DR, ks * P:(ks + 1) * P],
                             qfp[hb][0:DR, q0:q0 + 512],
                             start=False, stop=True, tile_position=(0, 0))
                    mm = t.matmul(scp[idx % 2][:, 512:1024],
                                  k_pe[DR:P, ks * P:(ks + 1) * P],
                                  qfp[hb][DR:P, q0 + 512:q0 + 1024],
                                  start=False, stop=True,
                                  tile_position=(DR, 0))
                else:
                    for o, wd in cks:
                        mm = t.matmul(scp[idx % 2][:, o:o + wd],
                                      k_pe[0:DR, ks * P:(ks + 1) * P],
                                      qfp[hb][0:DR, q0 + o:q0 + o + wd],
                                      start=False, stop=True)
                scm.inc(mm)
                _w(sc, scm)
                if j == 0:
                    _w(sc, bsd)
                if idx >= 4:
                    _w(sc, ladc, idx - 3)
                exd.inc(sc.activation(atb[:, idx % 4, 0:HF],
                                      scp[idx % 2][:, 0:HF],
                                      AF.Exp, bias=bias[:, 0:1],
                                      scale=SCALE))

            def emit_av(j):
                qh, ks = items[j]
                idx = h * 2 * NKS + j
                ph = h * 2 + qh
                _w(t, exd, idx + 1)
                if ks == 0 and ph >= 1:
                    _w(t, aod, ph)
                    if ph >= 2:
                        _w(t, rbd, ph - 1)
                for o, wd in cks:
                    mm = t.matmul(avp[:, o:o + wd], Vt[:, ks, :],
                                  atb[:, idx % 4, o:o + wd],
                                  start=(ks == 0), stop=(ks == NKS - 1))
                if ks == NKS - 1:
                    avm.inc(mm)

            def emit_lacc(j):
                qh, ks = items[j]
                idx = h * 2 * NKS + j
                ph = h * 2 + qh
                _w(v, exd, idx + 1)
                if ks == 0:
                    if ph >= 2:
                        _w(v, lrm, ph - 1)
                    ladc.inc(v.tensor_copy(lac[:, qh % 2, 0:HF],
                                           atb[:, idx % 4, 0:HF]))
                else:
                    _w(v, ladc, ladc.v)
                    ladc.inc(v.tensor_tensor(lac[:, qh % 2, 0:HF],
                                             lac[:, qh % 2, 0:HF],
                                             atb[:, idx % 4, 0:HF], ALU.add))

            def emit_post_a(qh, hh):
                q0 = qh * HF
                ph = hh * 2 + qh
                _w(v, avm, ph + 1)
                if hh >= 2 and qh == 0:
                    spl.wait(v, hh - 1)
                aod.inc(v.tensor_copy(aob[hh % 2][:, q0:q0 + HF],
                                      avp[:, 0:HF]))

            def emit_post_b(qh, hh, dst):
                ph = hh * 2 + qh
                _w(t, ladc, (ph + 1) * NKS)
                _w(t, bsd)
                if dst is avp:
                    _w(t, aod, ph + 1)
                elif ph >= 1:
                    _w(t, rbd, ph)
                for o, wd in cks:
                    mm = t.matmul(dst[0:1, o:o + wd], ones_f[:],
                                  lac[:, qh % 2, o:o + wd],
                                  start=True, stop=True)
                lrm.inc(mm)
                _w(sc, lrm, ph + 1)
                if ph >= 1:
                    _w(sc, bcd, ph)
                rcd.inc(sc.activation(rl[:, 0:HF], dst[0:1, 0:HF], AF.Ln))
                _w(sc, rcd)
                rcd.inc(sc.activation(rl[:, 0:HF], rl[0:1, 0:HF], AF.Exp,
                                      scale=-1.0))

            def emit_post_c(qh, hh, dst):
                q0 = qh * HF
                ph = hh * 2 + qh
                _w(t, rcd, 2 * (ph + 1))
                for o, wd in cks:
                    mm = t.matmul(dst[:, o:o + wd], onesr_f[:],
                                  rl[0:1, o:o + wd], start=True, stop=True)
                bcd.inc(mm)
                _w(sc, bcd, ph + 1)
                if ph >= 1:
                    _w(sc, aom, ph)
                rbd.inc(sc.copy(rb[:], dst[:, 0:HF]))
                _w(v, rbd, ph + 1)
                aom.inc(v.tensor_tensor(aob[hh % 2][:, q0:q0 + HF],
                                        aob[hh % 2][:, q0:q0 + HF],
                                        rb[:, 0:HF], ALU.mult))
                if qh == 1:
                    _w(s, aom, 2 * hh + 2)
                    spl.dma(s, d_aosp[hh], aob[hh % 2][:])

            due = {}
            for j in range(len(items) + 2):
                if j < len(items):
                    emit_nope(j)
                    emit_pe(j)
                if j >= 2:
                    emit_av(j - 2)
                    emit_lacc(j - 2)
                    if items[j - 2][1] == NKS - 1:
                        qh = items[j - 2][0]
                        emit_post_a(qh, h)
                        if qh == 0:
                            due[j + 2] = lambda qh=qh, hh=h: emit_post_b(
                                qh, hh, lp)
                            due[j + 4] = lambda qh=qh, hh=h: emit_post_c(
                                qh, hh, lp)
                        else:
                            carry_posts.append(
                                lambda qh=qh, hh=h: emit_post_b(qh, hh, avp))
                            carry_posts.append(
                                lambda qh=qh, hh=h: emit_post_c(qh, hh, avp))
                for fn in (due.pop(j, None),):
                    if fn is not None:
                        fn()
            if h == NH - 1:
                while carry_posts:
                    carry_posts.pop(0)()
        spl.wait(s)
        wlc.wait(s)
        qld.wait(s)
        nc.all_engine_barrier()
        free_to(base)
        nc.clear_and_free_semaphores(nc.allocated_since(_snap))
        nc.all_engine_barrier()

    if BUILD_PHASES >= 4:
        _snap = nc.snapshot_sems()
        ao = alloc("d_ao", [P, NH, T], BF16)
        wot = alloc("wot_sb", [P, 16, H], BF16)
        xo = [alloc(f"d_xo{i}", [P, H], F32) for i in range(2)]
        h1b = [alloc(f"d_h1b{i}", [P, H], F32) for i in range(2)]
        h1bf = [alloc(f"d_h1bf{i}", [P, H], BF16) for i in range(2)]
        h1tb = [alloc(f"d_h1tb{i}", [P, H], BF16) for i in range(2)]
        dstat = alloc("d_stat", [P, 8], F32)
        dsqj = alloc("d_sqj", [P, H], BF16)

        wo_ps = [alloc_ps(f"wo_ps{i}", [P, H // 2], F32) for i in range(2)]
        trd_ps = alloc_ps("trd_ps", [P, H], BF16)

        aol = GDma(nc, "d_aol", 16)
        wol = GDma(nc, "d_wol", 16)
        xol = GDma(nc, "d_xol", 2)
        wom = Ctr(nc, "d_wom")
        res = Ctr(nc, "d_res")
        lns = Ctr(nc, "d_lns")
        lnv = Ctr(nc, "d_lnv")
        htr = Ctr(nc, "d_htr")
        hev = Ctr(nc, "d_hev")
        dch = Ctr(nc, "d_dch")
        scd = Ctr(nc, "d_scd")
        hsp = GDma(nc, "d_hsp", 4)
        hspt = GDma(nc, "d_hspt", 2)

        for hv in range(NH):
            aol.dma(s, ao[:, hv, :], d_aosp[hv])
            wol.dma(g, wot[:, hv, :], d_wot[hv])
        xol.dma(s, xo[0][:], d_xo[0:P, :])

        def emit_tr(tt):
            _w(t, scd, 3 * (tt + 1))
            if tt >= 1:
                _w(t, hev, tt)
            for dc in range(16):
                mm = t.transpose(trd_ps[:, dc * P:(dc + 1) * P],
                                 h1bf[tt % 2][:, dc * P:(dc + 1) * P], ident[:])
            htr.inc(mm)
            _w(v, htr)
            if tt >= 2:
                hspt.wait(v, tt - 1)
            hev.inc(v.tensor_copy(h1tb[tt % 2][:], trd_ps[:]))
            _w(s, hev)
            hspt.dma(s, d_h1t[tt], h1tb[tt % 2][:])

        for tt in range(NQT):
            if tt + 1 < NQT:
                if tt >= 1:
                    _w(s, res, 2 * tt)
                xol.dma(s, xo[(tt + 1) % 2][:],
                        d_xo[(tt + 1) * P:(tt + 2) * P, :])
            for half in range(2):
                if tt >= 1:
                    _w(t, res, 2 * tt + half - 1)
                for hv in range(NH):
                    if tt == 0 and half == 0:
                        aol.wait(t, hv + 1)
                        wol.wait(t, hv + 1)
                    for ow in range(2):
                        o0 = half * 1024 + ow * 512
                        mm = t.matmul(wo_ps[half][:, ow * 512:(ow + 1) * 512],
                                      ao[:, hv, tt * P:(tt + 1) * P],
                                      wot[:, hv, o0:o0 + 512],
                                      start=(hv == 0), stop=(hv == 15))
                wom.inc(mm)
                _w(v, wom)
                if half == 0:
                    xol.wait(v, tt + 1)
                    if tt >= 2:
                        hsp.wait(v, tt - 1)
                res.inc(v.tensor_add(h1b[tt % 2][:, half * 1024:(half + 1) * 1024],
                                     wo_ps[half][:],
                                     xo[tt % 2][:, half * 1024:(half + 1) * 1024]))
            _w(v, res)
            dch.inc(v.reduce_sum(dstat[:, 0:1], h1b[tt % 2][:], axis=AX.X))
            _w(sc, res, 2 * (tt + 1))
            scd.inc(sc.activation(dsqj[:], h1b[tt % 2][:], AF.Square,
                                  accum_out=dstat[:, 2:3]))
            _w(v, dch)
            dch.inc(v.tensor_scalar(dstat[:, 1:2], dstat[:, 0:1], 1.0 / H,
                                    None, ALU.mult))
            _w(v, dch)
            dch.inc(v.tensor_tensor(dstat[:, 5:6], dstat[:, 1:2],
                                    dstat[:, 0:1], ALU.mult))
            _w(v, scd)
            _w(v, dch)
            dch.inc(v.tensor_tensor(dstat[:, 6:7], dstat[:, 2:3],
                                    dstat[:, 5:6], ALU.subtract))
            dch.inc(v.tensor_scalar(h1b[tt % 2][:], h1b[tt % 2][:],
                                    dstat[:, 1:2], None, ALU.subtract))
            _w(sc, dch)
            scd.inc(sc.activation(dstat[:, 3:4], dstat[:, 6:7], AF.Ln,
                                  bias=eps_l[:, 0:1], scale=1.0 / H))
            _w(sc, scd)
            if tt >= 1:
                _w(sc, lns, tt)
            lnv.inc(sc.activation(dstat[:, 4:5], dstat[:, 3:4], AF.Exp,
                                  scale=-0.5))
            _w(v, lnv)
            lns.inc(v.tensor_scalar(h1b[tt % 2][:], h1b[tt % 2][:],
                                    dstat[:, 4:5], None, ALU.mult))
            _w(sc, lns, tt + 1)
            scd.inc(sc.copy(h1bf[tt % 2][:], h1b[tt % 2][:]))
            _w(s, lns, tt + 1)
            hsp.dma(s, d_h1[tt], h1b[tt % 2][:])
            if tt >= 1:
                emit_tr(tt - 1)
        emit_tr(NQT - 1)

        hsp.wait(s)
        hspt.wait(s)
        aol.wait(s)
        wol.wait(s)
        xol.wait(s)
        nc.all_engine_barrier()
        free_to(base)
        nc.clear_and_free_semaphores(nc.allocated_since(_snap))
        nc.all_engine_barrier()

    _snap = nc.snapshot_sems()
    oss = GDma(nc, "e_oss", 2)
    if BUILD_PHASES >= 5:
        NTT = T // 512
        h1t = [alloc(f"e_h1t{i}", [P, 16, 512], BF16) for i in range(2)]
        gw = [alloc(f"e_gw{i}", [P, 16, P], BF16) for i in range(2)]
        uw = [alloc(f"e_uw{i}", [P, 16, P], BF16) for i in range(2)]
        wdb = [alloc(f"e_wd{i}", [P, 2, 1024], BF16) for i in range(3)]
        fuse = alloc("e_fuse", [P, 64, 512], BF16)
        silu = [alloc(f"e_silu{i}", [P, 512], F32) for i in range(2)]
        h1r = [alloc(f"e_h1r{i}", [P, H], F32) for i in range(2)]
        ob = [alloc(f"e_ob{i}", [P, H], F32) for i in range(2)]
        oc = alloc("e_oc", [P, H], F32)
        estat = alloc("e_stat", [P, 8], F32)
        esqj = alloc("e_sqj", [P, H], BF16)

        gup_ps = [alloc_ps(f"e_g{i}", [P, 512], F32) for i in range(2)]
        uup_ps = [alloc_ps(f"e_u{i}", [P, 512], F32) for i in range(2)]
        dn_ps = [alloc_ps(f"e_d{i}", [P, 1024], F32) for i in range(2)]

        htl = GDma(nc, "e_htl", 8)
        gul = GDma(nc, "e_gul", 4)
        wdl = GDma(nc, "e_wdl", 3)
        hrl = GDma(nc, "e_hrl", 4)
        gum = Ctr(nc, "e_gum")
        sld = Ctr(nc, "e_sld")
        fsd = Ctr(nc, "e_fsd")
        dnm = Ctr(nc, "e_dnm")
        dwm = Ctr(nc, "e_dwm")
        dev = Ctr(nc, "e_dev")
        lnc = Ctr(nc, "e_lnc")
        lsc = Ctr(nc, "e_lsc")
        ech = Ctr(nc, "e_ech")
        esc = Ctr(nc, "e_esc")

        for j in range(4):
            htl.dma(s, h1t[0][:, :, j * P:(j + 1) * P],
                    d_h1t[j].rearrange("p (dc f) -> p dc f", dc=16))
        gul.dma(g, gw[0][:], d_wg[0])
        gul.dma(g, uw[0][:], d_wu[0])
        hrl.dma(s, h1r[0][:], d_h1[0])
        hrl.dma(s, h1r[1][:], d_h1[1])

        n_sweep = 0
        fsd_at = {}

        for tt in range(NTT):
            if tt + 1 < NTT:
                _w(s, gum, 64 * tt)
                for j in range(4):
                    htl.dma(s, h1t[(tt + 1) % 2][:, :, j * P:(j + 1) * P],
                            d_h1t[(tt + 1) * 4 + j].rearrange(
                                "p (dc f) -> p dc f", dc=16))
            if tt >= 1:
                _w(s, dev, dev.v)
                hrl.dma(s, h1r[0][:], d_h1[tt * 4 + 0])
                hrl.dma(s, h1r[1][:], d_h1[tt * 4 + 1])

            for iw in range(64):
                it = tt * 64 + iw
                ic = iw
                if it + 1 < NTT * 64:
                    if it >= 1:
                        _w(g, gum, it)
                    gul.dma(g, gw[(it + 1) % 2][:], d_wg[(iw + 1) % 64])
                    gul.dma(g, uw[(it + 1) % 2][:], d_wu[(iw + 1) % 64])
                gul.wait(t, 2 * (it + 1))
                htl.wait(t, 4 * (tt + 1))
                if it >= 2:
                    _w(t, sld, it - 1)
                    _w(t, fsd, it - 1)
                for kc in range(16):
                    t.matmul(gup_ps[it % 2][:],
                             gw[it % 2][:, kc, :],
                             h1t[tt % 2][:, kc, :],
                             start=(kc == 0), stop=(kc == 15))
                for kc in range(16):
                    mm = t.matmul(uup_ps[it % 2][:],
                                  uw[it % 2][:, kc, :],
                                  h1t[tt % 2][:, kc, :],
                                  start=(kc == 0), stop=(kc == 15))
                gum.inc(mm)
                _w(sc, gum)
                if it >= 2:
                    _w(sc, fsd, it - 1)
                sld.inc(sc.activation(silu[it % 2][:], gup_ps[it % 2][:],
                                      AF.Silu, scale=sg_sb[:, ic:ic + 1]))
                _w(v, sld)
                if it % 64 < 2 and tt >= 1 and E_STOP >= 3:
                    _w(v, dev, 8 * tt)
                fsd.inc(v.scalar_tensor_tensor(
                    fuse[:, ic, :], uup_ps[it % 2][:],
                    su_sb[:, ic:ic + 1], silu[it % 2][:],
                    ALU.mult, ALU.mult))
                fsd_at[ic] = fsd.v

            if E_STOP < 2:
                continue
            for pp in range(2):
                for oh in range(2):
                    swi = n_sweep
                    n_sweep += 1
                    for icg in range(32):
                        gli = swi * 32 + icg
                        if gli >= 3:
                            _w(g, dnm, gli - 2)
                        wdl.dma(g, wdb[gli % 3][:],
                                d_wd[icg * 2:(icg + 1) * 2,
                                     :, oh * 1024:(oh + 1) * 1024].rearrange(
                                    "j p hh -> p j hh"))
                    for icg in range(32):
                        gli = swi * 32 + icg
                        _w(t, fsd, fsd_at[min(icg * 2 + 1, 63)])
                        wdl.wait(t, gli + 1)
                        if icg == 0 and swi >= 1 and E_STOP >= 3:
                            _w(t, dev, 2 * swi)
                        for ici in range(2):
                            ic = icg * 2 + ici
                            for j in range(2):
                                tsl = pp * 2 + j
                                t.matmul(dn_ps[j][:, 0:512],
                                         fuse[:, ic, tsl * P:(tsl + 1) * P],
                                         wdb[gli % 3][:, ici, 0:512],
                                         start=(ic == 0), stop=(ic == 63))
                                mm = t.matmul(
                                    dn_ps[j][:, 512:1024],
                                    fuse[:, ic, tsl * P:(tsl + 1) * P],
                                    wdb[gli % 3][:, ici, 512:1024],
                                    start=(ic == 0), stop=(ic == 63))
                        dnm.inc(mm)
                    if E_STOP < 3:
                        continue
                    for j in range(2):
                        tsl = pp * 2 + j
                        gt = tt * 4 + tsl
                        _w(v, dnm)
                        hrl.wait(v)
                        if gt >= 2:
                            oss.wait(v, gt - 1)
                            _w(v, lsc, gt - 1)
                        ech.inc(v.scalar_tensor_tensor(
                            ob[j][:, oh * 1024:(oh + 1) * 1024],
                            dn_ps[j][:, :], 1.0,
                            sd_bc[:, oh * 1024:(oh + 1) * 1024],
                            ALU.mult, ALU.mult))
                        _w(v, ech)
                        dev.inc(v.tensor_add(
                            ob[j][:, oh * 1024:(oh + 1) * 1024],
                            ob[j][:, oh * 1024:(oh + 1) * 1024],
                            h1r[j][:, oh * 1024:(oh + 1) * 1024]))
                if E_STOP < 3:
                    continue
                for j in range(2):
                    tsl = pp * 2 + j
                    gt = tt * 4 + tsl
                    if gt >= 1:
                        _w(v, lsc, gt)
                    _w(v, dev)
                    ech.inc(v.reduce_sum(estat[:, 0:1], ob[j][:], axis=AX.X))
                    _w(v, ech)
                    ech.inc(v.tensor_scalar(estat[:, 1:2], estat[:, 0:1],
                                            1.0 / H, None, ALU.mult))
                    _w(v, ech)
                    lnc.inc(v.tensor_scalar(oc[:], ob[j][:], estat[:, 1:2],
                                            None, ALU.subtract))
                    _w(sc, lnc)
                    esc.inc(sc.activation(esqj[:], oc[:], AF.Square,
                                          accum_out=estat[:, 2:3]))
                    _w(sc, esc)
                    esc.inc(sc.activation(estat[:, 3:4], estat[:, 2:3],
                                          AF.Ln, bias=eps_l[:, 0:1],
                                          scale=1.0 / H))
                    _w(sc, esc)
                    lsc.inc(sc.activation(estat[:, 4:5], estat[:, 3:4],
                                          AF.Exp, scale=-0.5))
                    _w(v, lsc)
                    lnc.inc(v.tensor_scalar(ob[j][:], oc[:], estat[:, 4:5],
                                            None, ALU.mult))
                    _w(s, lnc)
                    r0 = tt * 512 + tsl * P
                    oss.dma(s, d_out[r0:r0 + P, :], ob[j][:])
                if pp == 0:
                    _w(s, dev, dev.v)
                    hrl.dma(s, h1r[0][:], d_h1[tt * 4 + 2])
                    hrl.dma(s, h1r[1][:], d_h1[tt * 4 + 3])

        oss.wait(s)
        htl.wait(s)
        gul.wait(s)
        wdl.wait(s)
        hrl.wait(s)
        nc.all_engine_barrier()
        free_to(base)

    free_to(0)
    return nc



def _host_layouts(inputs):
    x = np.asarray(inputs["L0_attn_hidden_states"], dtype=np.float32)
    wq_a = np.asarray(inputs["L0_attn_wq_a"], dtype=np.float32)
    q_norm_w = np.asarray(inputs["L0_attn_q_norm_w"], dtype=np.float32)
    wq_b = np.asarray(inputs["L0_attn_wq_b"], dtype=np.float32)
    wkv_a = np.asarray(inputs["L0_attn_wkv_a"], dtype=np.float32)
    kv_norm_w = np.asarray(inputs["L0_attn_kv_norm_w"], dtype=np.float32)
    wkv_b = np.asarray(inputs["L0_attn_wkv_b"], dtype=np.float32)
    wo = np.asarray(inputs["L0_attn_wo"], dtype=np.float32)
    w_gate = np.asarray(inputs["L0_ffn_w_gate"]).astype(np.int8)
    w_up = np.asarray(inputs["L0_ffn_w_up"]).astype(np.int8)
    w_down = np.asarray(inputs["L0_ffn_w_down"]).astype(np.int8)
    s_gate = np.asarray(inputs["L0_ffn_s_gate"], dtype=np.float32)
    s_up = np.asarray(inputs["L0_ffn_s_up"], dtype=np.float32)
    s_down = np.asarray(inputs["L0_ffn_s_down"], dtype=np.float32)

    wq_aT = np.ascontiguousarray(wq_a.T.reshape(16, P, QL))
    wq_bT = wq_b.T
    wqbn = np.stack([
        np.ascontiguousarray(
            wq_bT[:, h * DQ:h * DQ + DN].reshape(12, P, DN).transpose(1, 0, 2))
        for h in range(NH)], 0)
    wqbp_cols = np.concatenate(
        [wq_bT[:, h * DQ + DN:(h + 1) * DQ] for h in range(NH)], 1)
    wqbp = np.ascontiguousarray(
        wqbp_cols.reshape(12, P, NH * DR).transpose(1, 0, 2))
    wkv_aT = np.ascontiguousarray(wkv_a.T.reshape(16, P, 576))
    wkv_bT = wkv_b.T
    wkvb = np.stack([
        np.ascontiguousarray(
            wkv_bT[:, h * 256:(h + 1) * 256].reshape(4, P, 256).transpose(
                1, 0, 2))
        for h in range(NH)], 0)
    woT = np.ascontiguousarray(wo.T.reshape(16, P, H))
    wg_sw = np.ascontiguousarray(
        w_gate.reshape(16, P, 64, P).transpose(2, 1, 0, 3))
    wu_sw = np.ascontiguousarray(
        w_up.reshape(16, P, 64, P).transpose(2, 1, 0, 3))
    wd_r = np.ascontiguousarray(w_down.reshape(64, P, H))
    sg_t = np.ascontiguousarray(s_gate.reshape(64, P).T)
    su_t = np.ascontiguousarray(s_up.reshape(64, P).T)
    sd_r = np.ascontiguousarray(s_down.reshape(1, H))
    qnw = np.ascontiguousarray(q_norm_w.reshape(1, QL))
    kvnw = np.ascontiguousarray(kv_norm_w.reshape(1, KVL))
    ident = np.eye(P, dtype=np.float32)

    inv = 1.0 / 10000.0 ** (np.arange(0, DR, 2, dtype=np.float32) / DR)
    tpos = np.arange(S, dtype=np.float32)
    f = np.outer(tpos, inv)
    cosf = np.cos(f).astype(np.float32)
    sinf = np.sin(f).astype(np.float32)

    def tile_tab(a):
        n = a.shape[0] // P
        return np.ascontiguousarray(a.reshape(n, P, 32).transpose(1, 0, 2))

    def tile_tab_rep(a):
        n = a.shape[0] // P
        r = a.reshape(n, P, 32).transpose(1, 0, 2)[:, :, None, :]
        return np.ascontiguousarray(np.broadcast_to(r, (P, n, NH, 32)))

    shared = dict(
        wqa=wq_aT, wqbn=wqbn, wqbp=wqbp, wkva=wkv_aT, wkvb=wkvb, wot=woT,
        qnw=qnw, kvnw=kvnw, wg=wg_sw, wu=wu_sw, wd=wd_r, sg=sg_t, su=su_t,
        sd=sd_r, ident=ident,
        cosk=tile_tab(cosf).astype(ml_dtypes.bfloat16),
        sink=tile_tab(sinf).astype(ml_dtypes.bfloat16),
    )

    in_maps = []
    for c in range(N_CORES):
        b = c // 4
        qoff = (c % 4) * T
        m = dict(shared)
        m["x_full"] = np.ascontiguousarray(x[b])
        m["x_own"] = np.ascontiguousarray(x[b, qoff:qoff + T])
        m["cosq"] = tile_tab_rep(cosf[qoff:qoff + T])
        m["sinq"] = tile_tab_rep(sinf[qoff:qoff + T])
        in_maps.append(m)
    return in_maps


def kernel(**inputs):
    in_maps = _host_layouts(inputs)
    nc = build_program()
    res = run_bass_kernel_spmd(nc, in_maps, core_ids=list(range(N_CORES)))
    outs = [r["out"] for r in res.results]
    full = np.stack(outs, 0).reshape(B, S, H)
    return full.astype(np.float32)


# version 19
# speedup vs baseline: 1.1042x; 1.1042x over previous
import numpy as np
import ml_dtypes
import concourse.bass as bass
import concourse.mybir as mybir
from concourse.bass_utils import run_bass_kernel_spmd

F32 = mybir.dt.float32
BF16 = mybir.dt.bfloat16
I8 = mybir.dt.int8
AF = mybir.ActivationFunctionType
ALU = mybir.AluOpType
AX = mybir.AxisListType

P = 128
B = 2
S = 8192
T = 2048
H = 2048
NH = 16
QL = 1536
KVL = 512
DN = 128
DR = 64
DV = 128
DQ = DN + DR
INTER = 8192
SCALE = float(1.0 / np.sqrt(DQ))
RMS_EPS = 1e-6
LN_EPS = 1e-5
B_OFF = 50.0
N_CORES = 8

BUILD_PHASES = 99
E_STOP = 3
SCRATCH_KIND = "Internal"
TAPS = ()


class Ctr:
    def __init__(self, nc, name):
        self.h = nc.alloc_semaphore(name)
        self.v = 0

    def inc(self, inst, k=1):
        inst.then_inc(self.h, k)
        self.v += k
        return inst


def _w(eng, sem, val=None):
    eng.wait_ge(sem.h, sem.v if val is None else val)


class GDma:

    def __init__(self, nc, name, nslots=2):
        self.s = [nc.alloc_semaphore(f"{name}_{j}") for j in range(nslots)]
        self.n = 0
        self.nslots = nslots

    def dma(self, eng, dst, src):
        j = self.n % self.nslots
        prev = 16 * (self.n // self.nslots)
        if prev > 0:
            eng.wait_ge(self.s[j], prev)
        inst = eng.dma_start(dst, src).then_inc(self.s[j], 16)
        self.n += 1
        return inst

    def wait(self, eng, k=None):
        k = self.n if k is None else k
        if k <= 0:
            return
        for j in range(self.nslots):
            cnt = (k - j + self.nslots - 1) // self.nslots
            if cnt > 0:
                eng.wait_ge(self.s[j], 16 * cnt)


def build_program():
    nc = bass.Bass("TRN2", target_bir_lowering=False, debug=False)
    t = nc.tensor
    sc = nc.scalar
    v = nc.vector
    g = nc.gpsimd
    s = nc.sync

    d_xf = nc.dram_tensor("x_full", [S, H], F32, kind="ExternalInput").ap()
    d_xo = nc.dram_tensor("x_own", [T, H], F32, kind="ExternalInput").ap()
    d_wqa = nc.dram_tensor("wqa", [16, P, QL], F32, kind="ExternalInput").ap()
    d_wqbn = nc.dram_tensor("wqbn", [NH, P, 12, DN], F32, kind="ExternalInput").ap()
    d_wqbp = nc.dram_tensor("wqbp", [P, 12, NH * DR], F32, kind="ExternalInput").ap()
    d_wkva = nc.dram_tensor("wkva", [16, P, 576], F32, kind="ExternalInput").ap()
    d_wkvb = nc.dram_tensor("wkvb", [NH, P, 4, 256], F32, kind="ExternalInput").ap()
    d_wot = nc.dram_tensor("wot", [16, P, H], F32, kind="ExternalInput").ap()
    d_qnw = nc.dram_tensor("qnw", [1, QL], F32, kind="ExternalInput").ap()
    d_kvnw = nc.dram_tensor("kvnw", [1, KVL], F32, kind="ExternalInput").ap()
    d_wg = nc.dram_tensor("wg", [64, P, 16, P], I8, kind="ExternalInput").ap()
    d_wu = nc.dram_tensor("wu", [64, P, 16, P], I8, kind="ExternalInput").ap()
    d_wd = nc.dram_tensor("wd", [64, P, H], I8, kind="ExternalInput").ap()
    d_sg = nc.dram_tensor("sg", [P, 64], F32, kind="ExternalInput").ap()
    d_su = nc.dram_tensor("su", [P, 64], F32, kind="ExternalInput").ap()
    d_sd = nc.dram_tensor("sd", [1, H], F32, kind="ExternalInput").ap()
    d_cosk = nc.dram_tensor("cosk", [P, S // P, 32], BF16, kind="ExternalInput").ap()
    d_sink = nc.dram_tensor("sink", [P, S // P, 32], BF16, kind="ExternalInput").ap()
    d_cosq = nc.dram_tensor("cosq", [P, T // P, NH, 32], F32,
                            kind="ExternalInput").ap()
    d_sinq = nc.dram_tensor("sinq", [P, T // P, NH, 32], F32,
                            kind="ExternalInput").ap()
    d_id = nc.dram_tensor("ident", [P, P], F32, kind="ExternalInput").ap()

    d_qfn = nc.dram_tensor("qfn_sp", [NH, P, T], BF16, kind=SCRATCH_KIND).ap()
    d_qfp = nc.dram_tensor("qfp_sp", [T // P, DR, NH, P], BF16,
                           kind=SCRATCH_KIND).ap()
    d_aosp = nc.dram_tensor("attn_sp", [NH, P, T], BF16, kind=SCRATCH_KIND).ap()
    d_kvc = nc.dram_tensor("kvc_sp", [P, 4, S], BF16, kind=SCRATCH_KIND).ap()
    d_kpe = nc.dram_tensor("kpe_sp", [DR, S], BF16, kind=SCRATCH_KIND).ap()
    d_h1 = nc.dram_tensor("h1_sp", [T // P, P, H], F32, kind=SCRATCH_KIND).ap()
    d_h1t = nc.dram_tensor("h1t_sp", [T // P, P, H], BF16, kind=SCRATCH_KIND).ap()

    d_tap = {}
    if "kvc" in TAPS:
        d_tap["kvc"] = nc.dram_tensor("tap_kvc", [P, 4, S], BF16,
                                      kind="ExternalOutput").ap()
    if "kpe" in TAPS:
        d_tap["kpe"] = nc.dram_tensor("tap_kpe", [DR, S], BF16,
                                      kind="ExternalOutput").ap()
    if "qlora" in TAPS:
        d_tap["qlora"] = nc.dram_tensor("tap_qlora", [P, 12, T], BF16,
                                        kind="ExternalOutput").ap()

    d_out = nc.dram_tensor("out", [T, H], F32, kind="ExternalOutput").ap()

    NKS = S // P
    NQT = T // P

    ctx_stack = []

    def alloc(*args):
        cm = nc.sbuf_tensor(*args)
        hnd = cm.__enter__()
        ctx_stack.append(cm)
        return hnd

    def alloc_ps(*args):
        cm = nc.psum_tensor(*args)
        hnd = cm.__enter__()
        ctx_stack.append(cm)
        return hnd

    def free_to(n):
        while len(ctx_stack) > n:
            ctx_stack.pop().__exit__(None, None, None)

    tap_sem = Ctr(nc, "tap")

    ident = alloc("ident_bf", [P, P], BF16)
    ones_bf = alloc("ones_bf", [P, 1], BF16)
    ones_f = alloc("ones_f", [P, 1], F32)
    onesr_f = alloc("onesr_f", [1, P], F32)
    sd_bc = alloc("sd_bc", [P, H], F32)
    sg_sb = alloc("sg_sb", [P, 64], F32)
    su_sb = alloc("su_sb", [P, 64], F32)
    eps_r = alloc("eps_r", [P, 1], F32)
    eps_l = alloc("eps_l", [P, 1], F32)

    ld = Ctr(nc, "ld")
    gld = Ctr(nc, "gld")
    cst = Ctr(nc, "cst")

    g.dma_start(ident[:], d_id).then_inc(gld.h, 16)
    gld.v += 16
    with (nc.sbuf_tensor("c_sd", [1, H], F32) as c_sd,
          nc.psum_tensor("c_ps", [P, 512], F32) as c_ps):
        for dst, src in ((c_sd[:], d_sd),
                         (sg_sb[:], d_sg), (su_sb[:], d_su)):
            s.dma_start(dst, src).then_inc(ld.h, 16)
            ld.v += 16
        v.memset(ones_bf[:], 1.0)
        v.memset(ones_f[:], 1.0)
        v.memset(eps_r[:], RMS_EPS)
        v.memset(eps_l[:], LN_EPS)
        cst.inc(v.memset(onesr_f[:], 1.0))
        _w(t, ld)
        _w(t, gld)
        _w(t, cst)
        for dst, src, n in ((sd_bc, c_sd, H),):
            for j in range(0, n, 512):
                wdt = min(512, n - j)
                _w(t, cst)
                cst.inc(t.matmul(c_ps[:, 0:wdt], onesr_f[:], src[0:1, j:j + wdt],
                                 start=True, stop=True))
                _w(v, cst)
                cst.inc(v.tensor_copy(dst[:, j:j + wdt], c_ps[:, 0:wdt]))
        nc.all_engine_barrier()
    consts_cst = cst.v

    base = len(ctx_stack)

    if BUILD_PHASES >= 2:
        _snap = nc.snapshot_sems()
        q_lora = alloc("q_lora", [P, 12, T], BF16)
        qnw_bc = alloc("qnw_bc", [P, QL], F32)
        c_qnw = alloc("b_qnw", [1, QL], F32)
        wqa = alloc("wqa_sb", [P, 16, QL], BF16)
        xw = [alloc(f"b_xw{i}", [P, H], BF16) for i in range(3)]
        xT = [alloc(f"b_xT{i}", [P, 16, P], BF16) for i in range(3)]
        qltmp = [alloc(f"qltmp{i}", [P, QL], F32) for i in range(2)]
        qltok = [alloc(f"qltok{i}", [P, QL], BF16) for i in range(2)]
        stat = alloc("b_stat", [P, 8], F32)
        sqj = alloc("b_sqj", [P, 512], BF16)

        ql_ps = [alloc_ps(f"ql_ps{i}", [P, 512], F32) for i in range(2)]
        trx_ps = [alloc_ps(f"trx_ps{i}", [P, 2048], BF16) for i in range(2)]
        trb_ps = alloc_ps("trb_ps", [P, QL], BF16)

        xl = GDma(nc, "b_xl", 3)
        wl = Ctr(nc, "b_wl")
        xtr = Ctr(nc, "b_xtr")
        tev = Ctr(nc, "b_tev")
        qlm = Ctr(nc, "b_qlm")
        qcp = Ctr(nc, "b_qcp")
        rsd = Ctr(nc, "b_rsd")
        vap = Ctr(nc, "b_vap")
        qtr = Ctr(nc, "b_qtr")
        qev = Ctr(nc, "b_qev")
        sch = Ctr(nc, "b_sch")
        vch = Ctr(nc, "b_vch")

        cs2 = Ctr(nc, "b_cs2")
        wlg = Ctr(nc, "b_wlg")
        vap_at = {}

        def emit_qltr(i):
            _w(t, vap, vap_at[i])
            if i >= 1:
                _w(t, qev, i)
            for j in range(12):
                mm = t.transpose(trb_ps[:, j * P:(j + 1) * P],
                                 qltok[i % 2][:, j * P:(j + 1) * P], ident[:])
            qtr.inc(mm)
            _w(v, qtr)
            qev.inc(v.tensor_copy(
                q_lora[:, :, i * P:(i + 1) * P],
                trb_ps.ap().rearrange("p (j f) -> p j f", j=12)))
        g.dma_start(wqa[:], d_wqa.rearrange("a p b -> p a b")).then_inc(wlg.h, 16)
        wlg.v += 16
        s.dma_start(c_qnw[:], d_qnw).then_inc(wl.h, 16)
        wl.v += 16
        _w(t, wl, wl.v)
        _w(t, wlg, wlg.v)
        for seg in range(3):
            _w(t, cs2)
            cs2.inc(t.matmul(ql_ps[0][:], onesr_f[:],
                             c_qnw[0:1, seg * 512:(seg + 1) * 512],
                             start=True, stop=True))
            _w(v, cs2)
            cs2.inc(v.tensor_copy(qnw_bc[:, seg * 512:(seg + 1) * 512],
                                  ql_ps[0][:]))

        def xpath(i):
            bx = i % 3
            if i >= 3:
                _w(g, xtr, i - 2)
            xl.dma(g, xw[bx][:], d_xo[i * P:(i + 1) * P, :])
            xl.wait(t, i + 1)
            if i >= 2:
                _w(t, tev, i - 1)
            if i == 0:
                _w(t, ld)
                _w(t, cst, consts_cst)
            for dc in range(16):
                mm = t.transpose(trx_ps[i % 2][:, dc * P:(dc + 1) * P],
                                 xw[bx][:, dc * P:(dc + 1) * P], ident[:])
            xtr.inc(mm)
            _w(v, xtr, i + 1)
            if i >= 3:
                _w(v, qlm, 3 * (i - 2))
            tev.inc(v.tensor_copy(
                xT[bx][:],
                trx_ps[i % 2].ap().rearrange("p (dc f) -> p dc f", dc=16)))

        xpath(0)
        xpath(1)
        for i in range(NQT):
            bx = i % 3
            if i + 2 < NQT:
                xpath(i + 2)
            _w(t, tev, i + 1)
            if i == 0:
                _w(t, cs2, 6)
            for seg in range(3):
                sg_i = i * 3 + seg
                if sg_i >= 2:
                    _w(t, qcp, sg_i - 1)
                for kc in range(16):
                    mm = t.matmul(ql_ps[sg_i % 2][:], xT[bx][:, kc, :],
                                  wqa[:, kc, seg * 512:(seg + 1) * 512],
                                  start=(kc == 0), stop=(kc == 15))
                qlm.inc(mm)
                _w(sc, qlm)
                if i >= 2 and seg == 0:
                    _w(sc, vap, i - 1)
                sc.copy(qltmp[i % 2][:, seg * 512:(seg + 1) * 512],
                        ql_ps[sg_i % 2][:])
                _w(sc, qcp)
                qcp.inc(sc.activation(sqj[:], ql_ps[sg_i % 2][:], AF.Square,
                                      accum_out=stat[:, seg:seg + 1]))
            _w(sc, qcp)
            if i >= 1:
                _w(sc, vch, i)
            sch.inc(sc.activation(stat[:, 3:4], stat[:, 0:1], AF.Identity,
                                  bias=stat[:, 1:2]))
            _w(sc, sch)
            sch.inc(sc.activation(stat[:, 4:5], stat[:, 3:4], AF.Identity,
                                  bias=stat[:, 2:3]))
            _w(sc, sch)
            sch.inc(sc.activation(stat[:, 5:6], stat[:, 4:5], AF.Ln,
                                  bias=eps_r[:, 0:1], scale=1.0 / QL))
            _w(sc, sch)
            rsd.inc(sc.activation(stat[:, 6:7], stat[:, 5:6], AF.Exp,
                                  scale=-0.5))
            _w(v, rsd)
            if i == 0:
                _w(v, cs2, 6)
            if i >= 2:
                _w(v, qtr, i - 1)
            vch.inc(v.tensor_scalar(qltmp[i % 2][:], qltmp[i % 2][:],
                                    stat[:, 6:7], None, ALU.mult))
            _w(v, vch)
            vap.inc(v.tensor_tensor(qltok[i % 2][:], qltmp[i % 2][:], qnw_bc[:],
                                    ALU.mult))
            vap_at[i] = vap.v
            if i >= 1:
                emit_qltr(i - 1)

        emit_qltr(NQT - 1)

        if "qlora" in d_tap:
            _w(s, qev)
            s.dma_start(d_tap["qlora"], q_lora[:]).then_inc(tap_sem.h, 16)
            tap_sem.v += 16

        xl.wait(s)
        s.wait_ge(wl.h, wl.v)
        s.wait_ge(wlg.h, wlg.v)
        s.wait_ge(tap_sem.h, tap_sem.v)
        nc.all_engine_barrier()
        free_to(base + 1)

        wqbn = [alloc(f"wqbn{i}", [P, 12, DN], BF16) for i in range(2)]
        qfnb = [alloc(f"qfnb{i}", [P, T], BF16) for i in range(2)]
        wqbp = alloc("wqbp_sb", [P, 12, NH * DR], BF16)
        qptok = [alloc(f"qptok{i}", [P, NH * DR], BF16) for i in range(2)]
        qfpst = [alloc(f"qfpst{i}", [DR, NH * P], BF16) for i in range(2)]
        cosq = [alloc(f"cosq{i}", [P, NH, 32], F32) for i in range(2)]
        sinq = [alloc(f"sinq{i}", [P, NH, 32], F32) for i in range(2)]
        rope1 = alloc("p_rope1", [P, NH, 32], F32)
        rope2 = alloc("p_rope2", [P, NH, 32], F32)
        rope3 = alloc("p_rope3", [P, NH, 32], F32)
        rope4 = alloc("p_rope4", [P, NH, 32], F32)

        qn_ps = [alloc_ps(f"qn_ps{i}", [P, 512], F32) for i in range(2)]
        qp_ps = [alloc_ps(f"qp_ps{i}", [P, NH * DR], F32) for i in range(2)]
        trp_ps = alloc_ps("trp_ps", [DR, NH * P], BF16)

        wl = Ctr(nc, "p_wl")
        wbn = GDma(nc, "p_wbn", 2)
        tbl = GDma(nc, "p_tbl", 4)
        spn = GDma(nc, "p_spn", 2)
        spp = GDma(nc, "p_spp", 2)
        qnm = Ctr(nc, "p_qnm")
        qne = Ctr(nc, "p_qne")
        spl = Ctr(nc, "p_spl")
        qpm = Ctr(nc, "p_qpm")
        rpd = Ctr(nc, "p_rpd")
        ptr = Ctr(nc, "p_ptr")
        pev = Ctr(nc, "p_pev")
        pch = Ctr(nc, "p_pch")

        g.dma_start(wqbp[:], d_wqbp).then_inc(wl.h, 16)
        wl.v += 16
        wbn.dma(g, wqbn[0][:], d_wqbn[0])

        for h in range(NH):
            if h + 1 < NH:
                if h >= 1:
                    _w(g, qnm, h * (T // 512))
                wbn.dma(g, wqbn[(h + 1) % 2][:], d_wqbn[h + 1])
            if h == 0:
                _w(t, wl, 16)
            wbn.wait(t, h + 1)
            if h >= 2:
                spn.wait(t, h - 1)
            for win in range(T // 512):
                iw = h * (T // 512) + win
                if iw >= 2:
                    _w(t, qne, iw - 1)
                for kc in range(12):
                    mm = t.matmul(qn_ps[iw % 2][:],
                                  wqbn[h % 2][:, kc, :],
                                  q_lora[:, kc, win * 512:(win + 1) * 512],
                                  start=(kc == 0), stop=(kc == 11))
                qnm.inc(mm)
                _w(sc, qnm)
                qne.inc(sc.copy(qfnb[h % 2][:, win * 512:(win + 1) * 512],
                                qn_ps[iw % 2][:]))
            _w(s, qne)
            spn.dma(s, d_qfn[h], qfnb[h % 2][:])

        tbl.dma(s, cosq[0][:], d_cosq[:, 0, :, :])
        tbl.dma(s, sinq[0][:], d_sinq[:, 0, :, :])

        for ts in range(NQT):
            if ts + 1 < NQT:
                if ts >= 1:
                    _w(s, rpd, ts)
                tbl.dma(s, cosq[(ts + 1) % 2][:], d_cosq[:, ts + 1, :, :])
                tbl.dma(s, sinq[(ts + 1) % 2][:], d_sinq[:, ts + 1, :, :])
            _w(t, wl, 16)
            if ts >= 2:
                _w(t, rpd, ts - 1)
            for kc in range(12):
                t.matmul(qp_ps[ts % 2][:, 0:512],
                         q_lora[:, kc, ts * P:(ts + 1) * P],
                         wqbp[:, kc, 0:512], start=(kc == 0), stop=(kc == 11))
            for kc in range(12):
                mm = t.matmul(qp_ps[ts % 2][:, 512:1024],
                              q_lora[:, kc, ts * P:(ts + 1) * P],
                              wqbp[:, kc, 512:1024], start=(kc == 0),
                              stop=(kc == 11))
            qpm.inc(mm)
            _w(v, qpm)
            tbl.wait(v, 2 * (ts + 1))
            pe = qp_ps[ts % 2].ap().rearrange("p (nh d two) -> p nh two d",
                                              two=2, nh=NH)
            qp = qptok[ts % 2].ap().rearrange("p (nh d two) -> p nh two d",
                                              two=2, nh=NH)
            ce = cosq[ts % 2][:]
            se = sinq[ts % 2][:]
            v.tensor_tensor(rope1[:], pe[:, :, 0, :], ce, ALU.mult)
            pch.inc(v.tensor_tensor(rope2[:], pe[:, :, 1, :], se, ALU.mult))
            _w(v, pch)
            v.tensor_sub(qp[:, :, 0, :], rope1[:], rope2[:])
            v.tensor_tensor(rope3[:], pe[:, :, 0, :], se, ALU.mult)
            rpd.inc(v.tensor_tensor(rope4[:], pe[:, :, 1, :], ce, ALU.mult))
            _w(v, rpd)
            qpm.inc(v.tensor_add(qp[:, :, 1, :], rope3[:], rope4[:]))
            _w(t, qpm)
            if ts >= 2:
                _w(t, pev, ts - 1)
            for h in range(NH):
                mm = t.transpose(trp_ps[:, h * P:(h + 1) * P],
                                 qptok[ts % 2][:, h * DR:(h + 1) * DR], ident[:])
            ptr.inc(mm)
            _w(v, ptr)
            if ts >= 2:
                spp.wait(v, ts - 1)
            pev.inc(v.tensor_copy(qfpst[ts % 2][:], trp_ps[:]))
            _w(s, pev)
            spp.dma(s, d_qfp[ts],
                    qfpst[ts % 2].ap().rearrange("p (nh f) -> p nh f", nh=NH))

        spn.wait(s)
        spp.wait(s)
        s.wait_ge(wl.h, wl.v)
        tbl.wait(s)
        nc.all_engine_barrier()
        free_to(base)
        nc.clear_and_free_semaphores(nc.allocated_since(_snap))
        nc.all_engine_barrier()

    if BUILD_PHASES >= 1:
        _snap = nc.snapshot_sems()
        kv_c = alloc("kv_c", [P, 4, S], BF16)
        k_pe = alloc("k_pe", [P, S], BF16)
        kvw_bc = alloc("kvw_bc", [P, KVL], F32)
        c_kvnw = alloc("a_kvnw", [1, KVL], F32)
        wkva = alloc("wkva_sb", [P, 16, 576], BF16)
        cosk = alloc("cosk_sb", [P, S // P, 32], BF16)
        sink = alloc("sink_sb", [P, S // P, 32], BF16)
        xw = [alloc(f"xw{i}", [P, H], BF16) for i in range(3)]
        xT = [alloc(f"xT{i}", [P, 16, P], BF16) for i in range(3)]
        kvtok = [alloc(f"kvtok{i}", [P, KVL], BF16) for i in range(2)]
        kpet = [alloc(f"kpet{i}", [P, DR], BF16) for i in range(2)]
        tmpf = alloc("tmpf", [P, KVL], F32)
        rope1 = alloc("rope1", [P, 32], F32)
        rope2 = alloc("rope2", [P, 32], F32)
        rope3 = alloc("rope3", [P, 32], F32)
        rope4 = alloc("rope4", [P, 32], F32)
        stat = alloc("stat", [P, 4], F32)
        sqj = alloc("sqj", [P, 512], BF16)

        kv_ps = [alloc_ps(f"kv_ps{i}", [P, 576], F32) for i in range(2)]
        tr_ps = [alloc_ps(f"tr_ps{i}", [P, 2048], BF16) for i in range(2)]

        xl = GDma(nc, "a_xl", 3)
        wl = Ctr(nc, "a_wl")
        xtr = Ctr(nc, "a_xtr")
        tev = Ctr(nc, "a_tev")
        kvm = Ctr(nc, "a_kvm")
        tevx = Ctr(nc, "a_tevx")
        rsd = Ctr(nc, "a_rsd")
        sqd = Ctr(nc, "a_sqd")
        vps = Ctr(nc, "a_vps")
        vrd = Ctr(nc, "a_vrd")
        vch = Ctr(nc, "a_vch")

        cs2 = Ctr(nc, "a_cs2")
        wlg = Ctr(nc, "a_wlg")
        g.dma_start(wkva[:], d_wkva.rearrange("a p b -> p a b")).then_inc(wlg.h, 16)
        wlg.v += 16
        s.dma_start(cosk[:], d_cosk).then_inc(wl.h, 16)
        s.dma_start(sink[:], d_sink).then_inc(wl.h, 16)
        s.dma_start(c_kvnw[:], d_kvnw).then_inc(wl.h, 16)
        wl.v += 48
        _w(t, wl, wl.v)
        _w(t, wlg, wlg.v)
        _w(t, cst, consts_cst)
        cs2.inc(t.matmul(kv_ps[0][:, 0:512], onesr_f[:], c_kvnw[0:1, :],
                         start=True, stop=True))
        _w(v, cs2)
        cs2.inc(v.tensor_copy(kvw_bc[:], kv_ps[0][:, 0:512]))

        gctr = [0]
        gx = {}
        gt = {}
        tr_groups = []
        evx_at = {}

        def emit_trc(i):
            bx2 = i % 2
            k = gctr[0]
            gctr[0] += 1
            gt[i] = k
            _w(t, vrd, i + 1)
            if k >= 2:
                c2, v2 = tr_groups[k - 2]
                t.wait_ge(c2.h, v2)
            for j in range(4):
                t.transpose(tr_ps[k % 2][:, j * P:(j + 1) * P],
                            kvtok[bx2][:, j * P:(j + 1) * P], ident[:])
            mm = t.transpose(tr_ps[k % 2][0:DR, 4 * P:5 * P],
                             kpet[bx2][:], ident[:])
            xtr.inc(mm)
            _w(v, xtr)
            pos = i * P
            v.tensor_copy(kv_c[:, :, pos:pos + P],
                          tr_ps[k % 2].ap()[:, 0:512].rearrange(
                              "p (j f) -> p j f", j=4))
            v.tensor_copy(k_pe[0:DR, pos:pos + P],
                          tr_ps[k % 2].ap()[0:DR, 4 * P:5 * P])
            tev.inc(v.tensor_copy(k_pe[DR:P, pos:pos + P],
                                  tr_ps[k % 2].ap()[0:DR, 4 * P:5 * P]))
            tr_groups.append((tev, tev.v))

        def xpath(i):
            bx = i % 3
            if i >= 3:
                _w(g, xtr, gx[i - 3] + 1)
            xl.dma(g, xw[bx][:], d_xf[i * P:(i + 1) * P, :])
            k = gctr[0]
            gctr[0] += 1
            gx[i] = k
            xl.wait(t, i + 1)
            if k >= 2:
                c2, v2 = tr_groups[k - 2]
                t.wait_ge(c2.h, v2)
            if k == 0:
                _w(t, gld)
                _w(t, cst, consts_cst)
            for dc in range(16):
                mm = t.transpose(tr_ps[k % 2][:, dc * P:(dc + 1) * P],
                                 xw[bx][:, dc * P:(dc + 1) * P], ident[:])
            xtr.inc(mm)
            _w(sc, xtr, xtr.v)
            if i >= 3:
                _w(sc, kvm, i - 2)
            tevx.inc(sc.copy(
                xT[bx][:],
                tr_ps[k % 2].ap().rearrange("p (dc f) -> p dc f", dc=16)))
            tr_groups.append((tevx, tevx.v))
            evx_at[i] = tevx.v

        xpath(0)
        xpath(1)
        for i in range(NKS):
            bx3 = i % 3
            bx = i % 2
            if i + 2 < NKS:
                xpath(i + 2)
            _w(t, tevx, evx_at[i])
            if i == 0:
                _w(t, cs2, 2)
            if i >= 2:
                _w(t, sqd, 2 * i - 3)
                _w(t, vps, i - 1)
            for kc in range(16):
                t.matmul(kv_ps[bx][:, 0:512], xT[bx3][:, kc, :],
                         wkva[:, kc, 0:512], start=(kc == 0), stop=(kc == 15))
            for kc in range(16):
                mm = t.matmul(kv_ps[bx][:, 512:576], xT[bx3][:, kc, :],
                              wkva[:, kc, 512:576],
                              start=(kc == 0), stop=(kc == 15))
            kvm.inc(mm)
            if i >= 1:
                emit_trc(i - 1)
            _w(sc, kvm, i + 1)
            if i >= 1:
                _w(sc, vch, 2 * i - 1)
            sqd.inc(sc.activation(sqj[:], kv_ps[bx][:, 0:512], AF.Square,
                                  accum_out=stat[:, 0:1]))
            _w(sc, sqd)
            sqd.inc(sc.activation(stat[:, 1:2], stat[:, 0:1], AF.Ln,
                                  bias=eps_r[:, 0:1], scale=1.0 / KVL))
            _w(sc, sqd)
            rsd.inc(sc.activation(stat[:, 2:3], stat[:, 1:2], AF.Exp,
                                  scale=-0.5))
            _w(v, rsd)
            if i == 0:
                _w(v, wl, 48)
                _w(v, cs2, 2)
            if i >= 2:
                _w(v, xtr, gt[i - 2] + 1)
            vch.inc(v.tensor_scalar(tmpf[:], kv_ps[bx][:, 0:512],
                                    stat[:, 2:3], None, ALU.mult))
            _w(v, vch)
            v.tensor_tensor(kvtok[bx][:], tmpf[:], kvw_bc[:], ALU.mult)
            pe = kv_ps[bx].ap()[:, 512:576].rearrange("p (d two) -> p two d",
                                                      two=2)
            kp = kpet[bx].ap().rearrange("p (d two) -> p two d", two=2)
            ce = cosk[:, i, :]
            se = sink[:, i, :]
            v.tensor_tensor(rope1[:], pe[:, 0, :], ce, ALU.mult)
            vch.inc(v.tensor_tensor(rope2[:], pe[:, 1, :], se, ALU.mult))
            _w(v, vch)
            v.tensor_sub(kp[:, 0, :], rope1[:], rope2[:])
            v.tensor_tensor(rope3[:], pe[:, 0, :], se, ALU.mult)
            vps.inc(v.tensor_tensor(rope4[:], pe[:, 1, :], ce, ALU.mult))
            _w(v, vps)
            vrd.inc(v.tensor_add(kp[:, 1, :], rope3[:], rope4[:]))
        emit_trc(NKS - 1)

        if "kvc" in d_tap:
            _w(s, tev)
            _w(s, tevx)
            s.dma_start(d_tap["kvc"], kv_c[:]).then_inc(tap_sem.h, 16)
            tap_sem.v += 16
        if "kpe" in d_tap:
            _w(s, tev)
            s.dma_start(d_tap["kpe"], k_pe[:]).then_inc(tap_sem.h, 16)
            tap_sem.v += 16

        xl.wait(s)
        s.wait_ge(wl.h, wl.v)
        s.wait_ge(wlg.h, wlg.v)
        s.wait_ge(tap_sem.h, tap_sem.v)
        nc.all_engine_barrier()
        free_to(base + 2)
        nc.clear_and_free_semaphores(nc.allocated_since(_snap))
        nc.all_engine_barrier()

    if BUILD_PHASES >= 3:
        _snap = nc.snapshot_sems()
        kf = alloc("kf_sb", [P, S], BF16)
        Vt = alloc("V_sb", [P, NKS, DV], BF16)
        HF = T // 2
        NCK = (HF + 511) // 512
        qfn = [alloc(f"qfn{i}", [P, T], BF16) for i in range(2)]
        qfp = [alloc(f"qfp{i}", [P, T], BF16) for i in range(2)]
        atb = alloc("at_sb", [P, 4, HF], BF16)
        sqb = [alloc(f"sqb{i}", [P, 512], BF16) for i in range(2)]
        sqp = [alloc(f"sqp{i}", [DR, 512], BF16) for i in range(2)]
        rb = alloc("rb_sb", [P, HF], F32)
        lac = alloc("lac_sb", [P, 2, HF], F32)
        aob = [alloc(f"aob{i}", [P, T], BF16) for i in range(2)]
        rl = alloc("rl_sb", [1, HF], F32)
        bias = alloc("bias_sb", [P, 1], F32)
        mstat = alloc("mstat", [1, 8], F32)
        kpe2s = alloc("kpe2s", [DR, S], BF16)
        wkvb = [alloc(f"wkvb{i}", [P, 4, 256], BF16) for i in range(2)]

        scp = [alloc_ps(f"scp{i}", [P, max(HF, 512)], F32) for i in range(2)]
        avp = alloc_ps("avp", [P, max(HF, 512)], F32)
        lp = alloc_ps("lp", [P, max(HF, 512)], F32)

        wlc = GDma(nc, "c_wl", 2)
        qld = GDma(nc, "c_qld", 6)
        kfm = Ctr(nc, "c_kfm")
        kfe = Ctr(nc, "c_kfe")
        k2m = Ctr(nc, "c_k2m")
        k2d = Ctr(nc, "c_k2d")
        vvm = Ctr(nc, "c_vvm")
        vve = Ctr(nc, "c_vve")
        qsq = Ctr(nc, "c_qsq")
        bsd = Ctr(nc, "c_bsd")
        scm = Ctr(nc, "c_scm")
        exd = Ctr(nc, "c_exd")
        pat = Ctr(nc, "c_pat")
        ladc = Ctr(nc, "c_ladc")
        lrm = Ctr(nc, "c_lrm")
        rcd = Ctr(nc, "c_rcd")
        bcd = Ctr(nc, "c_bcd")
        rbd = Ctr(nc, "c_rbd")
        aod = Ctr(nc, "c_aod")
        avm = Ctr(nc, "c_avm")
        aom = Ctr(nc, "c_aom")
        cch = Ctr(nc, "c_cch")
        kq = Ctr(nc, "c_kq")
        vsq = Ctr(nc, "c_vsq")
        spl = GDma(nc, "c_spl", 2)
        carry_posts = []

        wlc.dma(g, wkvb[0][:], d_wkvb[0])
        qld.dma(s, qfn[0][:], d_qfn[0])
        qld.dma(s, qfp[0].ap()[0:DR].rearrange("p (n f) -> p n f", f=P),
                d_qfp[:, :, 0, :].rearrange("n p f -> p n f"))
        qld.dma(s, qfp[0].ap()[DR:P].rearrange("p (n f) -> p n f", f=P),
                d_qfp[:, :, 0, :].rearrange("n p f -> p n f"))

        for h in range(NH):
            hb = h % 2
            if h + 1 < NH:
                if h >= 1:
                    _w(g, vvm, h * (S // 512))
                wlc.dma(g, wkvb[(h + 1) % 2][:], d_wkvb[h + 1])
                if h >= 1:
                    _w(s, scm, h * 2 * NKS)
                    _w(s, vsq, vsq.v)
                qld.dma(s, qfn[(h + 1) % 2][:], d_qfn[h + 1])
                qld.dma(s,
                        qfp[(h + 1) % 2].ap()[0:DR].rearrange(
                            "p (n f) -> p n f", f=P),
                        d_qfp[:, :, h + 1, :].rearrange("n p f -> p n f"))
                qld.dma(s,
                        qfp[(h + 1) % 2].ap()[DR:P].rearrange(
                            "p (n f) -> p n f", f=P),
                        d_qfp[:, :, h + 1, :].rearrange("n p f -> p n f"))

            vsq_at = {}
            wlc.wait(t, h + 1)
            if h == 0:
                for tc in range(S // 512):
                    kq.inc(v.tensor_tensor(kpe2s[:, tc * 512:(tc + 1) * 512],
                                           k_pe[0:DR, tc * 512:(tc + 1) * 512],
                                           k_pe[0:DR, tc * 512:(tc + 1) * 512],
                                           ALU.mult))

            def emit_ones_k(tc):
                _w(t, vsq, vsq_at[tc])
                _w(t, k2d)
                if tc == 0 and h >= 1:
                    _w(t, rbd)
                t.matmul(lp[0:1, 0:512], ones_bf[:], sqb[tc % 2][:],
                         start=True, stop=False)
                k2m.inc(t.matmul(lp[0:1, 0:512], ones_bf[0:DR, :],
                                 kpe2s[:, tc * 512:(tc + 1) * 512],
                                 start=False, stop=True))
                _w(v, k2m)
                _w(v, k2d)
                if tc == 0:
                    k2d.inc(v.reduce_max(mstat[:, 0:1], lp[0:1, 0:512],
                                         axis=AX.X))
                else:
                    cch.inc(v.reduce_max(mstat[:, 1:2], lp[0:1, 0:512],
                                         axis=AX.X))
                    _w(v, cch)
                    k2d.inc(v.tensor_max(mstat[:, 0:1], mstat[:, 0:1],
                                         mstat[:, 1:2]))

            for tc in range(S // 512):
                it = h * (S // 512) + tc
                if tc == 1 and carry_posts:
                    carry_posts.pop(0)()
                if tc == max(2, min(4, S // 512 - 1)) and carry_posts:
                    carry_posts.pop(0)()
                if it >= 2:
                    _w(t, kfe, it - 1)
                if tc < 2 and h >= 1:
                    _w(t, exd)
                for kc in range(4):
                    mm = t.matmul(scp[tc % 2][:, 0:512], wkvb[hb][:, kc, 0:DN],
                                  kv_c[:, kc, tc * 512:(tc + 1) * 512],
                                  start=(kc == 0), stop=(kc == 3))
                kfm.inc(mm)
                _w(sc, kfm)
                if tc < 2 and h >= 1:
                    _w(sc, scm)
                kfe.inc(sc.copy(kf[:, tc * 512:(tc + 1) * 512],
                                scp[tc % 2][:, 0:512]))
                _w(v, kfe)
                if h == 0 and tc == 0:
                    _w(v, kq)
                if tc >= 2:
                    _w(v, k2m)
                elif h >= 1:
                    _w(v, k2m)
                vsq.inc(v.tensor_tensor(sqb[tc % 2][:],
                                        kf[:, tc * 512:(tc + 1) * 512],
                                        kf[:, tc * 512:(tc + 1) * 512],
                                        ALU.mult))
                vsq_at[tc] = vsq.v
                if tc >= 1:
                    emit_ones_k(tc - 1)
            emit_ones_k(S // 512 - 1)

            def emit_qn_a(qc):
                if qc == 0:
                    qld.wait(v, 3 * (h + 1))
                if qc < 2:
                    _w(v, k2m)
                else:
                    _w(v, k2m, k2m.v - 1)
                v.tensor_tensor(sqb[qc % 2][:],
                                qfn[hb][:, qc * 512:(qc + 1) * 512],
                                qfn[hb][:, qc * 512:(qc + 1) * 512], ALU.mult)
                vsq.inc(v.tensor_tensor(sqp[qc % 2][:],
                                        qfp[hb][0:DR, qc * 512:(qc + 1) * 512],
                                        qfp[hb][0:DR, qc * 512:(qc + 1) * 512],
                                        ALU.mult))

            def emit_qn_b(qc):
                _w(t, vsq)
                _w(t, k2d)
                t.matmul(lp[0:1, 0:512], ones_bf[:], sqb[qc % 2][:],
                         start=True, stop=False)
                k2m.inc(t.matmul(lp[0:1, 0:512], ones_bf[0:DR, :], sqp[qc % 2][:],
                                 start=False, stop=True))
                _w(v, k2m)
                _w(v, k2d)
                if qc == 0:
                    k2d.inc(v.reduce_max(mstat[:, 2:3], lp[0:1, 0:512],
                                         axis=AX.X))
                else:
                    cch.inc(v.reduce_max(mstat[:, 3:4], lp[0:1, 0:512],
                                         axis=AX.X))
                    _w(v, cch)
                    k2d.inc(v.tensor_max(mstat[:, 2:3], mstat[:, 2:3],
                                         mstat[:, 3:4]))

            def emit_bias():
                _w(v, k2d)
                if h >= 1:
                    _w(v, exd)
                k2d.inc(v.tensor_tensor(mstat[:, 4:5], mstat[:, 0:1],
                                        mstat[:, 2:3], ALU.mult))
                _w(sc, k2d)
                qsq.inc(sc.activation(mstat[:, 5:6], mstat[:, 4:5], AF.Sqrt,
                                      scale=1.0404))
                _w(t, qsq)
                _w(t, k2d)
                k2m.inc(t.matmul(lp[:, 0:1], onesr_f[:], mstat[0:1, 5:6],
                                 start=True, stop=True))
                _w(v, k2m)
                bsd.inc(v.tensor_scalar(bias[:], lp[:, 0:1], -SCALE, B_OFF,
                                        ALU.mult, ALU.add))

            NVG = S // 512
            NQC = T // 512
            if NVG >= 3 * NQC + 4:
                qn_sched = {}
                for qc in range(NQC):
                    qn_sched[3 * qc + 3] = [lambda qc=qc: emit_qn_a(qc)]
                    qn_sched[3 * qc + 4] = [lambda qc=qc: emit_qn_b(qc)]
                qn_sched[3 * NQC + 3] = [emit_bias]
            else:
                qn_sched = None

            for vg in range(NVG):
                it = h * NVG + vg
                if it >= 2:
                    _w(t, vve, it - 1)
                if vg < 2:
                    _w(t, kfe)
                for sl in range(4):
                    for kc in range(4):
                        mm = t.matmul(
                            scp[vg % 2][:, sl * P:(sl + 1) * P],
                            kv_c[:, kc, (vg * 4 + sl) * P:(vg * 4 + sl + 1) * P],
                            wkvb[hb][:, kc, DN:256],
                            start=(kc == 0), stop=(kc == 3))
                vvm.inc(mm)
                _w(v, vvm)
                vve.inc(v.tensor_copy(
                    Vt[:, vg * 4:(vg + 1) * 4, :],
                    scp[vg % 2].ap()[:, 0:512].rearrange("p (sl f) -> p sl f", sl=4)))
                if qn_sched:
                    for fn in qn_sched.pop(vg, []):
                        fn()
            if qn_sched is None:
                for qc in range(NQC):
                    emit_qn_a(qc)
                    emit_qn_b(qc)
                emit_bias()

            cks = [(o, min(512, HF - o)) for o in range(0, HF, 512)]
            items = [(qh, ks) for qh in range(2) for ks in range(NKS)]

            def emit_nope(j):
                qh, ks = items[j]
                q0 = qh * HF
                idx = h * 2 * NKS + j
                if idx >= 2:
                    _w(t, exd, idx - 1)
                if j < 2:
                    _w(t, vve)
                for o, wd in cks:
                    t.matmul(scp[idx % 2][:, o:o + wd],
                             kf[:, ks * P:(ks + 1) * P],
                             qfn[hb][:, q0 + o:q0 + o + wd],
                             start=True, stop=False)

            def emit_pe(j):
                qh, ks = items[j]
                q0 = qh * HF
                idx = h * 2 * NKS + j
                if HF == 1024:
                    t.matmul(scp[idx % 2][:, 0:512],
                             k_pe[0:DR, ks * P:(ks + 1) * P],
                             qfp[hb][0:DR, q0:q0 + 512],
                             start=False, stop=True, tile_position=(0, 0))
                    mm = t.matmul(scp[idx % 2][:, 512:1024],
                                  k_pe[DR:P, ks * P:(ks + 1) * P],
                                  qfp[hb][DR:P, q0 + 512:q0 + 1024],
                                  start=False, stop=True,
                                  tile_position=(DR, 0))
                else:
                    for o, wd in cks:
                        mm = t.matmul(scp[idx % 2][:, o:o + wd],
                                      k_pe[0:DR, ks * P:(ks + 1) * P],
                                      qfp[hb][0:DR, q0 + o:q0 + o + wd],
                                      start=False, stop=True)
                scm.inc(mm)
                _w(sc, scm)
                if j == 0:
                    _w(sc, bsd)
                if idx >= 4:
                    _w(sc, ladc, idx - 3)
                exd.inc(sc.activation(atb[:, idx % 4, 0:HF],
                                      scp[idx % 2][:, 0:HF],
                                      AF.Exp, bias=bias[:, 0:1],
                                      scale=SCALE))

            def emit_av(j):
                qh, ks = items[j]
                idx = h * 2 * NKS + j
                ph = h * 2 + qh
                _w(t, exd, idx + 1)
                if ks == 0 and ph >= 1:
                    _w(t, aod, ph)
                    if ph >= 2:
                        _w(t, rbd, ph - 1)
                for o, wd in cks:
                    mm = t.matmul(avp[:, o:o + wd], Vt[:, ks, :],
                                  atb[:, idx % 4, o:o + wd],
                                  start=(ks == 0), stop=(ks == NKS - 1))
                if ks == NKS - 1:
                    avm.inc(mm)

            def emit_lacc(j):
                qh, ks = items[j]
                idx = h * 2 * NKS + j
                ph = h * 2 + qh
                _w(v, exd, idx + 1)
                if ks == 0:
                    if ph >= 2:
                        _w(v, lrm, ph - 1)
                    ladc.inc(v.tensor_copy(lac[:, qh % 2, 0:HF],
                                           atb[:, idx % 4, 0:HF]))
                else:
                    _w(v, ladc, ladc.v)
                    ladc.inc(v.tensor_tensor(lac[:, qh % 2, 0:HF],
                                             lac[:, qh % 2, 0:HF],
                                             atb[:, idx % 4, 0:HF], ALU.add))

            def emit_post_a(qh, hh):
                q0 = qh * HF
                ph = hh * 2 + qh
                _w(v, avm, ph + 1)
                if hh >= 2 and qh == 0:
                    spl.wait(v, hh - 1)
                aod.inc(v.tensor_copy(aob[hh % 2][:, q0:q0 + HF],
                                      avp[:, 0:HF]))

            def emit_post_b(qh, hh, dst):
                ph = hh * 2 + qh
                _w(t, ladc, (ph + 1) * NKS)
                _w(t, bsd)
                if dst is avp:
                    _w(t, aod, ph + 1)
                elif ph >= 1:
                    _w(t, rbd, ph)
                for o, wd in cks:
                    mm = t.matmul(dst[0:1, o:o + wd], ones_f[:],
                                  lac[:, qh % 2, o:o + wd],
                                  start=True, stop=True)
                lrm.inc(mm)
                _w(sc, lrm, ph + 1)
                if ph >= 1:
                    _w(sc, bcd, ph)
                rcd.inc(sc.activation(rl[:, 0:HF], dst[0:1, 0:HF], AF.Ln))
                _w(sc, rcd)
                rcd.inc(sc.activation(rl[:, 0:HF], rl[0:1, 0:HF], AF.Exp,
                                      scale=-1.0))

            def emit_post_c(qh, hh, dst):
                q0 = qh * HF
                ph = hh * 2 + qh
                _w(t, rcd, 2 * (ph + 1))
                for o, wd in cks:
                    mm = t.matmul(dst[:, o:o + wd], onesr_f[:],
                                  rl[0:1, o:o + wd], start=True, stop=True)
                bcd.inc(mm)
                _w(sc, bcd, ph + 1)
                if ph >= 1:
                    _w(sc, aom, ph)
                rbd.inc(sc.copy(rb[:], dst[:, 0:HF]))
                _w(v, rbd, ph + 1)
                aom.inc(v.tensor_tensor(aob[hh % 2][:, q0:q0 + HF],
                                        aob[hh % 2][:, q0:q0 + HF],
                                        rb[:, 0:HF], ALU.mult))
                if qh == 1:
                    _w(s, aom, 2 * hh + 2)
                    spl.dma(s, d_aosp[hh], aob[hh % 2][:])

            due = {}
            for j in range(len(items) + 2):
                if j < len(items):
                    emit_nope(j)
                    emit_pe(j)
                if j >= 2:
                    emit_av(j - 2)
                    emit_lacc(j - 2)
                    if items[j - 2][1] == NKS - 1:
                        qh = items[j - 2][0]
                        emit_post_a(qh, h)
                        if qh == 0:
                            due[j + 2] = lambda qh=qh, hh=h: emit_post_b(
                                qh, hh, lp)
                            due[j + 4] = lambda qh=qh, hh=h: emit_post_c(
                                qh, hh, lp)
                        else:
                            carry_posts.append(
                                lambda qh=qh, hh=h: emit_post_b(qh, hh, avp))
                            carry_posts.append(
                                lambda qh=qh, hh=h: emit_post_c(qh, hh, avp))
                for fn in (due.pop(j, None),):
                    if fn is not None:
                        fn()
            if h == NH - 1:
                while carry_posts:
                    carry_posts.pop(0)()
        spl.wait(s)
        wlc.wait(s)
        qld.wait(s)
        nc.all_engine_barrier()
        free_to(base)
        nc.clear_and_free_semaphores(nc.allocated_since(_snap))
        nc.all_engine_barrier()

    if BUILD_PHASES >= 4:
        _snap = nc.snapshot_sems()
        ao = alloc("d_ao", [P, NH, T], BF16)
        wot = alloc("wot_sb", [P, 16, H], BF16)
        xo = [alloc(f"d_xo{i}", [P, H], F32) for i in range(2)]
        h1b = [alloc(f"d_h1b{i}", [P, H], F32) for i in range(2)]
        h1bf = [alloc(f"d_h1bf{i}", [P, H], BF16) for i in range(2)]
        h1tb = [alloc(f"d_h1tb{i}", [P, H], BF16) for i in range(2)]
        dstat = alloc("d_stat", [P, 8], F32)
        dsqj = alloc("d_sqj", [P, H], BF16)

        wo_ps = [alloc_ps(f"wo_ps{i}", [P, H // 2], F32) for i in range(2)]
        trd_ps = alloc_ps("trd_ps", [P, H], BF16)

        aol = GDma(nc, "d_aol", 16)
        wol = GDma(nc, "d_wol", 16)
        xol = GDma(nc, "d_xol", 2)
        wom = Ctr(nc, "d_wom")
        res = Ctr(nc, "d_res")
        lns = Ctr(nc, "d_lns")
        lnv = Ctr(nc, "d_lnv")
        htr = Ctr(nc, "d_htr")
        hev = Ctr(nc, "d_hev")
        dch = Ctr(nc, "d_dch")
        scd = Ctr(nc, "d_scd")
        hsp = GDma(nc, "d_hsp", 4)
        hspt = GDma(nc, "d_hspt", 2)

        for hv in range(NH):
            aol.dma(s, ao[:, hv, :], d_aosp[hv])
            wol.dma(g, wot[:, hv, :], d_wot[hv])
        xol.dma(s, xo[0][:], d_xo[0:P, :])

        def emit_tr(tt):
            _w(t, scd, 3 * (tt + 1))
            if tt >= 1:
                _w(t, hev, tt)
            for dc in range(16):
                mm = t.transpose(trd_ps[:, dc * P:(dc + 1) * P],
                                 h1bf[tt % 2][:, dc * P:(dc + 1) * P], ident[:])
            htr.inc(mm)
            _w(v, htr)
            if tt >= 2:
                hspt.wait(v, tt - 1)
            hev.inc(v.tensor_copy(h1tb[tt % 2][:], trd_ps[:]))
            _w(s, hev)
            hspt.dma(s, d_h1t[tt], h1tb[tt % 2][:])

        for tt in range(NQT):
            if tt + 1 < NQT:
                if tt >= 1:
                    _w(s, res, 2 * tt)
                xol.dma(s, xo[(tt + 1) % 2][:],
                        d_xo[(tt + 1) * P:(tt + 2) * P, :])
            for half in range(2):
                if tt >= 1:
                    _w(t, res, 2 * tt + half - 1)
                for hv in range(NH):
                    if tt == 0 and half == 0:
                        aol.wait(t, hv + 1)
                        wol.wait(t, hv + 1)
                    for ow in range(2):
                        o0 = half * 1024 + ow * 512
                        mm = t.matmul(wo_ps[half][:, ow * 512:(ow + 1) * 512],
                                      ao[:, hv, tt * P:(tt + 1) * P],
                                      wot[:, hv, o0:o0 + 512],
                                      start=(hv == 0), stop=(hv == 15))
                wom.inc(mm)
                _w(v, wom)
                if half == 0:
                    xol.wait(v, tt + 1)
                    if tt >= 2:
                        hsp.wait(v, tt - 1)
                res.inc(v.tensor_add(h1b[tt % 2][:, half * 1024:(half + 1) * 1024],
                                     wo_ps[half][:],
                                     xo[tt % 2][:, half * 1024:(half + 1) * 1024]))
            _w(v, res)
            dch.inc(v.reduce_sum(dstat[:, 0:1], h1b[tt % 2][:], axis=AX.X))
            _w(sc, res, 2 * (tt + 1))
            scd.inc(sc.activation(dsqj[:], h1b[tt % 2][:], AF.Square,
                                  accum_out=dstat[:, 2:3]))
            _w(v, dch)
            dch.inc(v.tensor_scalar(dstat[:, 1:2], dstat[:, 0:1], 1.0 / H,
                                    None, ALU.mult))
            _w(v, dch)
            dch.inc(v.tensor_tensor(dstat[:, 5:6], dstat[:, 1:2],
                                    dstat[:, 0:1], ALU.mult))
            _w(v, scd)
            _w(v, dch)
            dch.inc(v.tensor_tensor(dstat[:, 6:7], dstat[:, 2:3],
                                    dstat[:, 5:6], ALU.subtract))
            dch.inc(v.tensor_scalar(h1b[tt % 2][:], h1b[tt % 2][:],
                                    dstat[:, 1:2], None, ALU.subtract))
            _w(sc, dch)
            scd.inc(sc.activation(dstat[:, 3:4], dstat[:, 6:7], AF.Ln,
                                  bias=eps_l[:, 0:1], scale=1.0 / H))
            _w(sc, scd)
            if tt >= 1:
                _w(sc, lns, tt)
            lnv.inc(sc.activation(dstat[:, 4:5], dstat[:, 3:4], AF.Exp,
                                  scale=-0.5))
            _w(v, lnv)
            lns.inc(v.tensor_scalar(h1b[tt % 2][:], h1b[tt % 2][:],
                                    dstat[:, 4:5], None, ALU.mult))
            _w(sc, lns, tt + 1)
            scd.inc(sc.copy(h1bf[tt % 2][:], h1b[tt % 2][:]))
            _w(s, lns, tt + 1)
            hsp.dma(s, d_h1[tt], h1b[tt % 2][:])
            if tt >= 1:
                emit_tr(tt - 1)
        emit_tr(NQT - 1)

        hsp.wait(s)
        hspt.wait(s)
        aol.wait(s)
        wol.wait(s)
        xol.wait(s)
        nc.all_engine_barrier()
        free_to(base)
        nc.clear_and_free_semaphores(nc.allocated_since(_snap))
        nc.all_engine_barrier()

    _snap = nc.snapshot_sems()
    oss = GDma(nc, "e_oss", 2)
    if BUILD_PHASES >= 5:
        NTT = T // 512
        h1t = [alloc(f"e_h1t{i}", [P, 16, 512], BF16) for i in range(2)]
        gw = [alloc(f"e_gw{i}", [P, 16, P], BF16) for i in range(2)]
        uw = [alloc(f"e_uw{i}", [P, 16, P], BF16) for i in range(2)]
        wdb = [alloc(f"e_wd{i}", [P, 2, 1024], BF16) for i in range(3)]
        fuse = alloc("e_fuse", [P, 64, 512], BF16)
        silu = [alloc(f"e_silu{i}", [P, 512], F32) for i in range(2)]
        h1r = [alloc(f"e_h1r{i}", [P, H], F32) for i in range(2)]
        ob = [alloc(f"e_ob{i}", [P, H], F32) for i in range(2)]
        oc = alloc("e_oc", [P, H], F32)
        estat = alloc("e_stat", [P, 8], F32)
        esqj = alloc("e_sqj", [P, H], BF16)

        gup_ps = [alloc_ps(f"e_g{i}", [P, 512], F32) for i in range(2)]
        uup_ps = [alloc_ps(f"e_u{i}", [P, 512], F32) for i in range(2)]
        dn_ps = [alloc_ps(f"e_d{i}", [P, 1024], F32) for i in range(2)]

        htl = GDma(nc, "e_htl", 8)
        gul = GDma(nc, "e_gul", 4)
        wdl = GDma(nc, "e_wdl", 3)
        hrl = GDma(nc, "e_hrl", 4)
        gum = Ctr(nc, "e_gum")
        sld = Ctr(nc, "e_sld")
        fsd = Ctr(nc, "e_fsd")
        dnm = Ctr(nc, "e_dnm")
        dwm = Ctr(nc, "e_dwm")
        dev = Ctr(nc, "e_dev")
        lnc = Ctr(nc, "e_lnc")
        lsc = Ctr(nc, "e_lsc")
        ech = Ctr(nc, "e_ech")
        esc = Ctr(nc, "e_esc")

        for j in range(4):
            htl.dma(s, h1t[0][:, :, j * P:(j + 1) * P],
                    d_h1t[j].rearrange("p (dc f) -> p dc f", dc=16))
        gul.dma(g, gw[0][:], d_wg[0])
        gul.dma(g, uw[0][:], d_wu[0])
        hrl.dma(s, h1r[0][:], d_h1[0])
        hrl.dma(s, h1r[1][:], d_h1[1])

        n_sweep = 0
        fsd_at = {}

        for tt in range(NTT):
            if tt + 1 < NTT:
                _w(s, gum, 64 * tt)
                for j in range(4):
                    htl.dma(s, h1t[(tt + 1) % 2][:, :, j * P:(j + 1) * P],
                            d_h1t[(tt + 1) * 4 + j].rearrange(
                                "p (dc f) -> p dc f", dc=16))
            if tt >= 1:
                _w(s, dev, dev.v)
                hrl.dma(s, h1r[0][:], d_h1[tt * 4 + 0])
                hrl.dma(s, h1r[1][:], d_h1[tt * 4 + 1])

            for iw in range(64):
                it = tt * 64 + iw
                ic = iw
                if it + 1 < NTT * 64:
                    if it >= 1:
                        _w(g, gum, it)
                    gul.dma(g, gw[(it + 1) % 2][:], d_wg[(iw + 1) % 64])
                    gul.dma(g, uw[(it + 1) % 2][:], d_wu[(iw + 1) % 64])
                gul.wait(t, 2 * (it + 1))
                htl.wait(t, 4 * (tt + 1))
                if it >= 2:
                    _w(t, sld, it - 1)
                    _w(t, fsd, it - 1)
                for kc in range(16):
                    t.matmul(gup_ps[it % 2][:],
                             gw[it % 2][:, kc, :],
                             h1t[tt % 2][:, kc, :],
                             start=(kc == 0), stop=(kc == 15))
                for kc in range(16):
                    mm = t.matmul(uup_ps[it % 2][:],
                                  uw[it % 2][:, kc, :],
                                  h1t[tt % 2][:, kc, :],
                                  start=(kc == 0), stop=(kc == 15))
                gum.inc(mm)
                _w(sc, gum)
                if it >= 2:
                    _w(sc, fsd, it - 1)
                sld.inc(sc.activation(silu[it % 2][:], gup_ps[it % 2][:],
                                      AF.Silu, scale=sg_sb[:, ic:ic + 1]))
                _w(v, sld)
                if it % 64 < 2 and tt >= 1 and E_STOP >= 3:
                    _w(v, dev, 8 * tt)
                fsd.inc(v.scalar_tensor_tensor(
                    fuse[:, ic, :], uup_ps[it % 2][:],
                    su_sb[:, ic:ic + 1], silu[it % 2][:],
                    ALU.mult, ALU.mult))
                fsd_at[ic] = fsd.v

            if E_STOP < 2:
                continue
            for pp in range(2):
                for oh in range(2):
                    swi = n_sweep
                    n_sweep += 1
                    for icg in range(32):
                        gli = swi * 32 + icg
                        if gli >= 3:
                            _w(g, dnm, gli - 2)
                        wdl.dma(g, wdb[gli % 3][:],
                                d_wd[icg * 2:(icg + 1) * 2,
                                     :, oh * 1024:(oh + 1) * 1024].rearrange(
                                    "j p hh -> p j hh"))
                    for icg in range(32):
                        gli = swi * 32 + icg
                        _w(t, fsd, fsd_at[min(icg * 2 + 1, 63)])
                        wdl.wait(t, gli + 1)
                        if icg == 0 and swi >= 1 and E_STOP >= 3:
                            _w(t, dev, 2 * swi)
                        for ici in range(2):
                            ic = icg * 2 + ici
                            for j in range(2):
                                tsl = pp * 2 + j
                                t.matmul(dn_ps[j][:, 0:512],
                                         fuse[:, ic, tsl * P:(tsl + 1) * P],
                                         wdb[gli % 3][:, ici, 0:512],
                                         start=(ic == 0), stop=(ic == 63))
                                mm = t.matmul(
                                    dn_ps[j][:, 512:1024],
                                    fuse[:, ic, tsl * P:(tsl + 1) * P],
                                    wdb[gli % 3][:, ici, 512:1024],
                                    start=(ic == 0), stop=(ic == 63))
                        dnm.inc(mm)
                    if E_STOP < 3:
                        continue
                    for j in range(2):
                        tsl = pp * 2 + j
                        gt = tt * 4 + tsl
                        _w(v, dnm)
                        hrl.wait(v)
                        if gt >= 2:
                            oss.wait(v, gt - 1)
                            _w(v, lsc, gt - 1)
                        ech.inc(v.scalar_tensor_tensor(
                            ob[j][:, oh * 1024:(oh + 1) * 1024],
                            dn_ps[j][:, :], 1.0,
                            sd_bc[:, oh * 1024:(oh + 1) * 1024],
                            ALU.mult, ALU.mult))
                        _w(v, ech)
                        dev.inc(v.tensor_add(
                            ob[j][:, oh * 1024:(oh + 1) * 1024],
                            ob[j][:, oh * 1024:(oh + 1) * 1024],
                            h1r[j][:, oh * 1024:(oh + 1) * 1024]))
                if E_STOP < 3:
                    continue
                for j in range(2):
                    tsl = pp * 2 + j
                    gt = tt * 4 + tsl
                    if gt >= 1:
                        _w(v, lsc, gt)
                    _w(v, dev)
                    ech.inc(v.reduce_sum(estat[:, 0:1], ob[j][:], axis=AX.X))
                    _w(v, ech)
                    ech.inc(v.tensor_scalar(estat[:, 1:2], estat[:, 0:1],
                                            1.0 / H, None, ALU.mult))
                    _w(v, ech)
                    lnc.inc(v.tensor_scalar(oc[:], ob[j][:], estat[:, 1:2],
                                            None, ALU.subtract))
                    _w(sc, lnc)
                    esc.inc(sc.activation(esqj[:], oc[:], AF.Square,
                                          accum_out=estat[:, 2:3]))
                    _w(sc, esc)
                    esc.inc(sc.activation(estat[:, 3:4], estat[:, 2:3],
                                          AF.Ln, bias=eps_l[:, 0:1],
                                          scale=1.0 / H))
                    _w(sc, esc)
                    lsc.inc(sc.activation(estat[:, 4:5], estat[:, 3:4],
                                          AF.Exp, scale=-0.5))
                    _w(v, lsc)
                    lnc.inc(v.tensor_scalar(ob[j][:], oc[:], estat[:, 4:5],
                                            None, ALU.mult))
                    _w(s, lnc)
                    r0 = tt * 512 + tsl * P
                    oss.dma(s, d_out[r0:r0 + P, :], ob[j][:])
                if pp == 0:
                    _w(s, dev, dev.v)
                    hrl.dma(s, h1r[0][:], d_h1[tt * 4 + 2])
                    hrl.dma(s, h1r[1][:], d_h1[tt * 4 + 3])

        oss.wait(s)
        htl.wait(s)
        gul.wait(s)
        wdl.wait(s)
        hrl.wait(s)
        nc.all_engine_barrier()
        free_to(base)

    free_to(0)
    return nc



def _host_layouts(inputs):
    x = np.asarray(inputs["L0_attn_hidden_states"], dtype=np.float32)
    wq_a = np.asarray(inputs["L0_attn_wq_a"], dtype=np.float32)
    q_norm_w = np.asarray(inputs["L0_attn_q_norm_w"], dtype=np.float32)
    wq_b = np.asarray(inputs["L0_attn_wq_b"], dtype=np.float32)
    wkv_a = np.asarray(inputs["L0_attn_wkv_a"], dtype=np.float32)
    kv_norm_w = np.asarray(inputs["L0_attn_kv_norm_w"], dtype=np.float32)
    wkv_b = np.asarray(inputs["L0_attn_wkv_b"], dtype=np.float32)
    wo = np.asarray(inputs["L0_attn_wo"], dtype=np.float32)
    w_gate = np.asarray(inputs["L0_ffn_w_gate"]).astype(np.int8)
    w_up = np.asarray(inputs["L0_ffn_w_up"]).astype(np.int8)
    w_down = np.asarray(inputs["L0_ffn_w_down"]).astype(np.int8)
    s_gate = np.asarray(inputs["L0_ffn_s_gate"], dtype=np.float32)
    s_up = np.asarray(inputs["L0_ffn_s_up"], dtype=np.float32)
    s_down = np.asarray(inputs["L0_ffn_s_down"], dtype=np.float32)

    wq_aT = np.ascontiguousarray(wq_a.T.reshape(16, P, QL))
    wq_bT = wq_b.T
    wqbn = np.stack([
        np.ascontiguousarray(
            wq_bT[:, h * DQ:h * DQ + DN].reshape(12, P, DN).transpose(1, 0, 2))
        for h in range(NH)], 0)
    wqbp_cols = np.concatenate(
        [wq_bT[:, h * DQ + DN:(h + 1) * DQ] for h in range(NH)], 1)
    wqbp = np.ascontiguousarray(
        wqbp_cols.reshape(12, P, NH * DR).transpose(1, 0, 2))
    wkv_aT = np.ascontiguousarray(wkv_a.T.reshape(16, P, 576))
    wkv_bT = wkv_b.T
    wkvb = np.stack([
        np.ascontiguousarray(
            wkv_bT[:, h * 256:(h + 1) * 256].reshape(4, P, 256).transpose(
                1, 0, 2))
        for h in range(NH)], 0)
    woT = np.ascontiguousarray(wo.T.reshape(16, P, H))
    wg_sw = np.ascontiguousarray(
        w_gate.reshape(16, P, 64, P).transpose(2, 1, 0, 3))
    wu_sw = np.ascontiguousarray(
        w_up.reshape(16, P, 64, P).transpose(2, 1, 0, 3))
    wd_r = np.ascontiguousarray(w_down.reshape(64, P, H))
    sg_t = np.ascontiguousarray(s_gate.reshape(64, P).T)
    su_t = np.ascontiguousarray(s_up.reshape(64, P).T)
    sd_r = np.ascontiguousarray(s_down.reshape(1, H))
    qnw = np.ascontiguousarray(q_norm_w.reshape(1, QL))
    kvnw = np.ascontiguousarray(kv_norm_w.reshape(1, KVL))
    ident = np.eye(P, dtype=np.float32)

    inv = 1.0 / 10000.0 ** (np.arange(0, DR, 2, dtype=np.float32) / DR)
    tpos = np.arange(S, dtype=np.float32)
    f = np.outer(tpos, inv)
    cosf = np.cos(f).astype(np.float32)
    sinf = np.sin(f).astype(np.float32)

    def tile_tab(a):
        n = a.shape[0] // P
        return np.ascontiguousarray(a.reshape(n, P, 32).transpose(1, 0, 2))

    def tile_tab_rep(a):
        n = a.shape[0] // P
        r = a.reshape(n, P, 32).transpose(1, 0, 2)[:, :, None, :]
        return np.ascontiguousarray(np.broadcast_to(r, (P, n, NH, 32)))

    shared = dict(
        wqa=wq_aT, wqbn=wqbn, wqbp=wqbp, wkva=wkv_aT, wkvb=wkvb, wot=woT,
        qnw=qnw, kvnw=kvnw, wg=wg_sw, wu=wu_sw, wd=wd_r, sg=sg_t, su=su_t,
        sd=sd_r, ident=ident,
        cosk=tile_tab(cosf).astype(ml_dtypes.bfloat16),
        sink=tile_tab(sinf).astype(ml_dtypes.bfloat16),
    )

    in_maps = []
    for c in range(N_CORES):
        b = c // 4
        qoff = (c % 4) * T
        m = dict(shared)
        m["x_full"] = np.ascontiguousarray(x[b])
        m["x_own"] = np.ascontiguousarray(x[b, qoff:qoff + T])
        m["cosq"] = tile_tab_rep(cosf[qoff:qoff + T])
        m["sinq"] = tile_tab_rep(sinf[qoff:qoff + T])
        in_maps.append(m)
    return in_maps


def kernel(**inputs):
    in_maps = _host_layouts(inputs)
    nc = build_program()
    res = run_bass_kernel_spmd(nc, in_maps, core_ids=list(range(N_CORES)))
    outs = [r["out"] for r in res.results]
    full = np.stack(outs, 0).reshape(B, S, H)
    return full.astype(np.float32)
